# version 03
# baseline (speedup 1.0000x reference)
import numpy as np
from contextlib import ExitStack

import concourse.bass as bass
import concourse.mybir as mybir
from concourse.bass_utils import run_bass_kernel_spmd

F32 = mybir.dt.float32
BF16 = mybir.dt.bfloat16
AF = mybir.ActivationFunctionType
ALU = mybir.AluOpType
AX = mybir.AxisListType

D = 2048
T = 2048
NDC = 16
H = 16
NE = 8
RLO = 64
EPS = 1e-5
GOLD_C = 0.36787944117144233
GOLD_L = 0.2123179275482191
GOLD_U = 0.5
QSCALE_A = float(128) ** -0.5
CSCALE_B = 1.0 / float(np.sqrt(np.float32(2048.0)))
NEG_BIG = -1e30
DBG = {}

SEMS = ["pe", "dv", "ac", "cst", "xL", "wA", "wB", "wC", "wD",
        "wAp", "wBp", "wCp", "wDp",
        "qL0", "qL1", "oL", "stX", "qS", "qS1", "hS", "oS",
        "cwA", "cwB", "cwC", "cwD"]
DMA_SEMS = SEMS[3:]
WSEM = {0: "wA", 1: "wB", 2: "wC", 3: "wD"}
WSEMP = {0: "wAp", 1: "wBp", 2: "wCp", 3: "wDp"}
CWSEM = {0: "cwA", 1: "cwB", 2: "cwC", 3: "cwD"}


class KCtx:
    ENGS = ("t", "v", "a", "s", "p")

    def __init__(self, nc, ctx, names):
        self.nc = nc
        self.sem = {}
        self.cnt = {}
        for nm in names:
            self.sem[nm] = ctx.enter_context(nc.semaphore("S_" + nm))
            self.cnt[nm] = 0
        self.waited = {}

    def i(self, nm, instr, n=1):
        instr.then_inc(self.sem[nm], n)
        self.cnt[nm] += n
        return self.cnt[nm]

    def w(self, eng, ek, nm, tgt=None):
        if tgt is None:
            tgt = self.cnt[nm]
        if tgt <= 0:
            return
        key = (ek, nm)
        if self.waited.get(key, -1) >= tgt:
            return
        eng.wait_ge(self.sem[nm], tgt)
        self.waited[key] = tgt

    def barrier(self):
        for ek in self.ENGS:
            for nm in self.sem:
                self.waited[(ek, nm)] = self.cnt[nm]


class Buf:
    def __init__(self):
        self.rel = []

    def wait_free(self, k, eng, ek):
        for nm, tgt in self.rel:
            k.w(eng, ek, nm, tgt)
        self.rel = []

    def mark(self, nm, tgt):
        self.rel.append((nm, tgt))



def emit_ln_tile(k, blk, xt_ap, out_ap, scr):
    stats, mv, vv, rstd = scr["stats"], scr["mv"], scr["vv"], scr["rstd"]

    def _(v):
        k.w(v, "v", "ac")
        k.w(v, "v", "dv")
        ins = None
        for i in range(4):
            ins = v.bn_stats(stats[:, 6 * i:6 * i + 6],
                             xt_ap[:, 512 * i:512 * (i + 1)])
        k.i("dv", ins)
        k.w(v, "v", "dv")
        k.i("dv", v.bn_aggr(mv[:, :], stats[:, :]))
        k.w(v, "v", "dv")
        k.i("dv", v.tensor_scalar(vv[:, :], mv[:, 1:2], EPS, None, ALU.add))
    blk.vector(_)

    def _(sc):
        k.w(sc, "a", "dv")
        k.i("ac", sc.activation(rstd[:, :], vv[:, :], AF.Sqrt))
    blk.scalar(_)

    def _(v):
        k.w(v, "v", "ac")
        k.i("dv", v.reciprocal(rstd[:, :], rstd[:, :]))
        k.w(v, "v", "dv")
        k.i("dv", v.tensor_scalar(out_ap, xt_ap, mv[:, 0:1], rstd[:, 0:1],
                                  ALU.subtract, ALU.mult))
    blk.vector(_)
    return k.cnt["dv"]


def emit_transpose_tile(k, blk, src_bf16, hfm_write, tp_ps, tp_state, ident_bf,
                        affine=None):
    for dc in range(NDC):
        bank = dc % 2
        ps = tp_ps[bank]

        def _(t, bank=bank, ps=ps, dc=dc):
            if bank in tp_state:
                nm, v = tp_state[bank]
                k.w(t, "t", nm, v)
            k.i("pe", t.transpose(ps[:, 0:128],
                                  src_bf16[:, dc * 128:(dc + 1) * 128],
                                  ident_bf[:, :]))
        blk.tensor(_)

        if affine is None:
            def _(v, dc=dc, bank=bank, ps=ps):
                k.w(v, "v", "pe")
                tgt = k.i("dv", v.tensor_copy(hfm_write(dc), ps[:, 0:128]))
                tp_state[bank] = ("dv", tgt)
            blk.vector(_)
        else:
            gs_fn, bb_fn = affine

            def _(sc, dc=dc, bank=bank, ps=ps):
                k.w(sc, "a", "pe")
                tgt = k.i("ac", sc.activation(
                    hfm_write(dc), ps[:, 0:128],
                    AF.Identity, scale=gs_fn(dc), bias=bb_fn(dc)))
                tp_state[bank] = ("ac", tgt)
            blk.scalar(_)


def load_cb_raw(k, blk, src4d, cb, buf, wsem):
    def _(gp):
        buf.wait_free(k, gp, "p")
        for dcg in range(4):
            k.i(wsem, gp.dma_start(
                out=cb[:, dcg * 2048:(dcg + 1) * 2048].rearrange(
                    "p (dc f) -> p dc f", f=512),
                in_=src4d[:, dcg * 4:(dcg + 1) * 4, :]), 16)
    blk.gpsimd(_)
    return (wsem, k.cnt[wsem])


def drain_dmas(k, blk):
    def _(sy):
        for nm in DMA_SEMS:
            k.w(sy, "s", nm)
    blk.sync(_)



def phase_A1(nc, k, g, dram):
    with ExitStack() as ctx:
        E = ctx.enter_context
        xb = E(nc.sbuf_tensor("a1_xb", [128, 2048], F32))
        h_tm = E(nc.sbuf_tensor("a1_htm", [128, 2048], BF16))
        h_fm = E(nc.sbuf_tensor("a1_hfm", [128, 8192], BF16))
        cb0 = E(nc.sbuf_tensor("a1_cb0", [128, 8192], BF16))
        cb1 = E(nc.sbuf_tensor("a1_cb1", [128, 8192], BF16))
        esq = E(nc.sbuf_tensor("a1_esq", [128, 2048], BF16))
        murs = E(nc.sbuf_tensor("a1_murs", [1, 2048], BF16))
        rowf = E(nc.sbuf_tensor("a1_rowf", [1, 1536], F32))
        d1 = E(nc.sbuf_tensor("a1_d1", [128, 512], BF16))
        qo = E(nc.sbuf_tensor("a1_qo", [128, 1024], BF16))
        stats = E(nc.sbuf_tensor("a1_st", [128, 24], F32))
        mv = E(nc.sbuf_tensor("a1_mv", [128, 2], F32))
        vv = E(nc.sbuf_tensor("a1_vv", [128, 1], F32))
        rstd = E(nc.sbuf_tensor("a1_rs", [128, 1], F32))
        tp0 = E(nc.psum_tensor("a1_tp0", [128, 1024], BF16))
        tp1 = E(nc.psum_tensor("a1_tp1", [128, 1024], BF16))
        pq0 = E(nc.psum_tensor("a1_pq0", [128, 512], F32))
        pq1 = E(nc.psum_tensor("a1_pq1", [128, 512], F32))
        pv = E(nc.psum_tensor("a1_pv", [128, 512], F32))
        st1 = E(nc.psum_tensor("a1_s1", [1, 512], F32))
        st2 = E(nc.psum_tensor("a1_s2", [1, 512], F32))
        bps = E(nc.psum_tensor("a1_b", [128, 512], F32))
        blk = E(nc.Block("A1"))

        scr = {"stats": stats, "mv": mv, "vv": vv, "rstd": rstd}
        K_fm, V_tm = g["K_fm"], g["V_tm"]
        ident_bf = g["ident_bf"]
        ones_bf = g["ones_bf"]
        gam_a, bet_a = g["gam_a"], g["bet_a"]
        pqs = (pq0, pq1)
        cbbuf = [Buf(), Buf()]
        xb_buf = Buf()
        htm_buf = Buf()
        tp_state = {}
        wsrc = {w: dram[w].rearrange("(dc p) f -> p dc f", p=128)
                for w in ("wqt", "wkt", "wvt")}

        pipe = {}

        def S0(n, job):
            wname, cb, sem, tgt, head, scol, fi = job
            par = n % 2

            def _(t):
                k.w(t, "t", sem, tgt)
                k.w(t, "t", "dv", pipe["hfm_ready"])
                prev = pipe.get(("e", n - 2))
                if prev:
                    k.w(t, "t", "ac", prev)
                for dc in range(NDC):
                    ins = t.matmul(
                        pqs[par][:, :],
                        cb[:, dc * 512 + fi * 128:dc * 512 + (fi + 1) * 128],
                        h_fm[:, dc * 512:(dc + 1) * 512],
                        start=(dc == 0), stop=(dc == NDC - 1))
                k.i("pe", ins)
            blk.tensor(_)
            pipe[("mm", n)] = k.cnt["pe"]

        def S1(n, job):
            par = n % 2
            eo = par * 1024

            def _(sc):
                k.w(sc, "a", "pe", pipe[("mm", n)])
                prev = pipe.get(("apply", n - 2))
                if prev:
                    k.w(sc, "a", "dv", prev)
                k.i("ac", sc.activation(esq[:, eo:eo + 512], pqs[par][:, :],
                                        AF.Identity, scale=gam_a[:, 0:1],
                                        bias=bet_a[:, 0:1]))
                pipe[("e", n)] = k.cnt["ac"]
                k.w(sc, "a", "ac")
                k.i("ac", sc.activation(esq[:, eo + 512:eo + 1024],
                                        esq[:, eo:eo + 512], AF.Square))
            blk.scalar(_)
            pipe[("e2", n)] = k.cnt["ac"]

        def S2(n, job):
            par = n % 2
            eo = par * 1024

            def _(t):
                k.w(t, "t", "ac", pipe[("e2", n)])
                prev = pipe.get(("stfree", n - 1))
                if prev:
                    k.w(t, "t", "dv", prev)
                t.matmul(st1[:, :], ones_bf[:, 0:1],
                         esq[:, eo:eo + 512], start=True, stop=True)
                k.i("pe", t.matmul(st2[:, :], ones_bf[:, 0:1],
                                   esq[:, eo + 512:eo + 1024],
                                   start=True, stop=True))
            blk.tensor(_)
            pipe[("st", n)] = k.cnt["pe"]

        def S3(n, job):
            par = n % 2
            ro = 0

            def _(v):
                k.w(v, "v", "pe", pipe[("st", n)])
                k.w(v, "v", "dv")
                k.w(v, "v", "ac")
                k.i("dv", v.tensor_scalar(rowf[0:1, ro:ro + 512], st1[0:1, :],
                                          1.0 / 128, None, ALU.mult))
                k.w(v, "v", "dv")
                k.i("dv", v.tensor_tensor(rowf[0:1, ro + 1024:ro + 1536],
                                          rowf[0:1, ro:ro + 512],
                                          rowf[0:1, ro:ro + 512], ALU.mult))
                k.w(v, "v", "dv")
                k.i("dv", v.scalar_tensor_tensor(rowf[0:1, ro + 512:ro + 1024],
                                                 st2[0:1, :], 1.0 / 128,
                                                 rowf[0:1, ro + 1024:ro + 1536],
                                                 ALU.mult, ALU.subtract))
                pipe[("stfree", n)] = k.cnt["dv"]
                k.w(v, "v", "dv")
                k.i("dv", v.tensor_scalar(rowf[0:1, ro + 512:ro + 1024],
                                          rowf[0:1, ro + 512:ro + 1024],
                                          EPS, None, ALU.add))
            blk.vector(_)

            def _(sc):
                k.w(sc, "a", "dv")
                k.i("ac", sc.activation(rowf[0:1, ro + 512:ro + 1024],
                                        rowf[0:1, ro + 512:ro + 1024], AF.Sqrt))
            blk.scalar(_)

            mo = par * 1024

            def _(v):
                k.w(v, "v", "ac")
                k.i("dv", v.reciprocal(rowf[0:1, ro + 512:ro + 1024],
                                       rowf[0:1, ro + 512:ro + 1024]))
                k.w(v, "v", "dv")
                k.i("dv", v.tensor_copy(murs[0:1, mo:mo + 256],
                                        rowf[0:1, ro:ro + 256]))
                k.i("dv", v.tensor_copy(murs[0:1, mo + 256:mo + 512],
                                        rowf[0:1, ro + 512:ro + 768]))
                k.i("dv", v.tensor_copy(murs[0:1, mo + 512:mo + 768],
                                        rowf[0:1, ro + 256:ro + 512]))
                k.i("dv", v.tensor_copy(murs[0:1, mo + 768:mo + 1024],
                                        rowf[0:1, ro + 768:ro + 1024]))
            blk.vector(_)
            pipe[("murs", n)] = k.cnt["dv"]

        def S4(n, job):
            wname, cb, sem, tgt, head, scol, fi = job
            par = n % 2
            eo = par * 1024
            mo = par * 1024
            do = par * 256
            qoo = par * 512
            out_is_k = (wname == "wkt")
            for half in range(2):
                def _(t, half=half):
                    k.w(t, "t", "dv", pipe[("murs", n)])
                    prev = pipe.get(("apply_h", n, half - 1)) or pipe.get(
                        ("apply", n - 1))
                    if prev:
                        k.w(t, "t", "dv", prev)
                    k.i("pe", t.matmul(bps[:, :], ones_bf[0:1, 0:128],
                                       murs[0:1, mo + half * 512:mo + (half + 1) * 512],
                                       start=True, stop=True))
                blk.tensor(_)
                b_t = k.cnt["pe"]

                def _(v, half=half, b_t=b_t):
                    k.w(v, "v", "pe", b_t)
                    k.i("dv", v.tensor_tensor(
                        d1[:, do:do + 256],
                        esq[:, eo + half * 256:eo + (half + 1) * 256],
                        bps[:, 0:256], ALU.subtract))
                    k.w(v, "v", "dv")
                    if out_is_k:
                        dst = K_fm[:, head * 2048 + scol + half * 256:
                                   head * 2048 + scol + (half + 1) * 256]
                    else:
                        dst = qo[:, qoo + half * 256:qoo + (half + 1) * 256]
                        if half == 0:
                            prevq = pipe.get(("qspill", n - 2))
                            if prevq:
                                k.w(v, "v", "qS" if n % 2 == 0 else "qS1",
                                    prevq)
                    k.i("dv", v.tensor_tensor(dst, d1[:, do:do + 256],
                                              bps[:, 256:512], ALU.mult))
                blk.vector(_)
                pipe[("apply_h", n, half)] = k.cnt["dv"]
            pipe[("apply", n)] = k.cnt["dv"]

            if not out_is_k:
                qsn = "qS" if n % 2 == 0 else "qS1"

                def _(sy, qsn=qsn):
                    k.w(sy, "s", "dv", pipe[("apply", n)])
                    k.i(qsn, sy.dma_start(
                        out=dram["Qa"][head * 128:(head + 1) * 128,
                                       scol:scol + 512],
                        in_=qo[:, qoo:qoo + 512]), 16)
                blk.sync(_)
                pipe[("qspill", n)] = k.cnt[qsn]

        for s in range(4):
            scol = s * 512
            for ti in range(4):
                row0 = scol + ti * 128

                def _(sy, row0=row0):
                    xb_buf.wait_free(k, sy, "s")
                    k.i("xL", sy.dma_start(out=xb[:, :],
                                           in_=dram["hs"][row0:row0 + 128, :]), 16)
                blk.sync(_)
                xld = k.cnt["xL"]

                def _(v, xld=xld):
                    k.w(v, "v", "xL", xld)
                    htm_buf.wait_free(k, v, "v")
                blk.vector(_)
                t = emit_ln_tile(k, blk, xb[:, :], h_tm[:, :], scr)
                xb_buf.mark("dv", t)

                def hw(dc, ti=ti):
                    return h_fm[:, dc * 512 + ti * 128:dc * 512 + (ti + 1) * 128]

                def _(t_, tgt=t):
                    k.w(t_, "t", "dv", tgt)
                blk.tensor(_)
                emit_transpose_tile(k, blk, h_tm, hw, [tp0, tp1], tp_state,
                                    ident_bf)
                htm_buf.mark("pe", k.cnt["pe"])
            pipe.clear()
            pipe["hfm_ready"] = k.cnt["dv"]
            if k.cnt["qS"]:
                pipe[("qspill", -2)] = k.cnt["qS"]
            if k.cnt["qS1"]:
                pipe[("qspill", -1)] = k.cnt["qS1"]

            jobs = []
            for wi, wname in enumerate(("wqt", "wkt")):
                for cbg in range(4):
                    for fi in range(4):
                        jobs.append((wname, cbg, fi))
            N = len(jobs)
            jinfo = {}

            def emit_slot(n):
                if n < N:
                    wname, cbg, fi = jobs[n]
                    if fi == 0:
                        wi = 0 if wname == "wqt" else 1
                        cbp = (wi * 4 + cbg) % 2
                        cb = (cb0, cb1)[cbp]
                        sem, tgt = load_cb_raw(
                            k, blk,
                            wsrc[wname][:, :, cbg * 512:(cbg + 1) * 512],
                            cb, cbbuf[cbp], WSEMP[cbp])
                        jinfo[(wname, cbg)] = (cb, sem, tgt, cbp)
                    cb, sem, tgt, cbp = jinfo[(wname, cbg)]
                    job = (wname, cb, sem, tgt, cbg * 4 + fi, scol, fi)
                    jinfo[n] = job
                    S0(n, job)
                    if fi == 3:
                        cbbuf[cbp].mark("pe", k.cnt["pe"])
                if 0 <= n - 1 < N:
                    S2(n - 1, jinfo[n - 1])
                if 0 <= n - 2 < N:
                    S4(n - 2, jinfo[n - 2])
                if n < N:
                    S1(n, jinfo[n])
                if 0 <= n - 1 < N:
                    S3(n - 1, jinfo[n - 1])

            for n in range(N + 2):
                emit_slot(n)

            for cbg in range(4):
                cbp = cbg % 2
                cb = (cb0, cb1)[cbp]
                sem, tgt = load_cb_raw(
                    k, blk, wsrc["wvt"][:, :, cbg * 512:(cbg + 1) * 512],
                    cb, cbbuf[cbp], WSEMP[cbp])
                for tc in range(4):
                    tcg = s * 4 + tc

                    def _(t, tc=tc, cb=cb, sem=sem, tgt=tgt):
                        k.w(t, "t", sem, tgt)
                        k.w(t, "t", "dv", pipe["hfm_ready"])
                        prev = pipe.get("v_evac")
                        if prev:
                            k.w(t, "t", "ac", prev)
                        for dc in range(NDC):
                            ins = t.matmul(
                                pv[:, :],
                                h_fm[:, dc * 512 + tc * 128:
                                     dc * 512 + (tc + 1) * 128],
                                cb[:, dc * 512:(dc + 1) * 512],
                                start=(dc == 0), stop=(dc == NDC - 1))
                        k.i("pe", ins)
                    blk.tensor(_)
                    mm_t = k.cnt["pe"]

                    def _(sc, tcg=tcg, cbg=cbg, mm_t=mm_t):
                        k.w(sc, "a", "pe", mm_t)
                        k.i("ac", sc.activation(
                            V_tm[:, tcg * 2048 + cbg * 512:
                                 tcg * 2048 + (cbg + 1) * 512],
                            pv[:, :], AF.Identity))
                    blk.scalar(_)
                    pipe["v_evac"] = k.cnt["ac"]
                cbbuf[cbp].mark("pe", k.cnt["pe"])
        drain_dmas(k, blk)



def phase_A2(nc, k, g, dram):
    with ExitStack() as ctx:
        E = ctx.enter_context
        qh = E(nc.sbuf_tensor("a2_qh", [128, 1024], BF16))
        at = E(nc.sbuf_tensor("a2_at", [128, 1024], BF16))
        zi = E(nc.sbuf_tensor("a2_zi", [128, 512], F32))
        ofm = E(nc.sbuf_tensor("a2_ofm", [128, 8192], BF16))
        am = E(nc.sbuf_tensor("a2_am", [128, 2048], BF16))
        psc0 = E(nc.psum_tensor("a2_s0", [128, 512], F32))
        psc1 = E(nc.psum_tensor("a2_s1", [128, 512], F32))
        psz0 = E(nc.psum_tensor("a2_z0", [128, 512], F32))
        psz1 = E(nc.psum_tensor("a2_z1", [128, 512], F32))
        pso0 = E(nc.psum_tensor("a2_oo0", [128, 512], F32))
        pso1 = E(nc.psum_tensor("a2_oo1", [128, 512], F32))
        blk = E(nc.Block("A2"))

        K_fm, V_tm = g["K_fm"], g["V_tm"]
        ones_bf, eps_bf = g["ones_bf"], g["eps_bf"]
        psc = (psc0, psc1)
        pszs = (psz0, psz1)
        psos = (pso0, pso1)
        oz_evac = {}
        qh_buf = [Buf(), Buf()]
        ofm_buf = Buf()

        def _(gp):
            for m in range(4):
                k.i("cst", gp.dma_start(out=am[:, m * 512:(m + 1) * 512],
                                        in_=dram["amc"][m]), 16)
        blk.gpsimd(_)
        am_t = k.cnt["cst"]

        for b in range(4):
            J = 4 * b + 4
            for h in range(H):
                qp = h % 2
                hp = (b * H + h) % 2
                psz = pszs[hp]
                pso = psos[hp]
                qsem = "qL0" if qp == 0 else "qL1"

                def _(sa, h=h, b=b, qp=qp, qsem=qsem):
                    qh_buf[qp].wait_free(k, sa, "a")
                    k.i(qsem, sa.dma_start(
                        out=qh[:, qp * 512:(qp + 1) * 512],
                        in_=dram["Qa"][h * 128:(h + 1) * 128,
                                       b * 512:(b + 1) * 512]), 16)
                blk.scalar(_)
                q_t = k.cnt[qsem]

                sc_t = {}
                sq_done = [0] * J
                sq_sem = ["ac"] * J
                oz_pe = [0] * J

                def emit_sc(j):
                    def _(t, j=j):
                        k.w(t, "t", qsem, q_t)
                        k.w(t, "t", "cst", am_t)
                        k.i("pe", t.matmul(
                            psc[j % 2][:, :],
                            K_fm[:, h * 2048 + j * 128:h * 2048 + (j + 1) * 128],
                            qh[:, qp * 512:(qp + 1) * 512],
                            start=True, stop=True))
                        sc_t[j] = k.cnt["pe"]
                    blk.tensor(_)

                def emit_sq(j):
                    def _(sc, j=j):
                        if j >= 2:
                            k.w(sc, "a", "pe", oz_pe[j - 2])
                        k.w(sc, "a", "pe", sc_t[j])
                        k.i("ac", sc.activation(
                            at[:, (j % 2) * 512:(j % 2 + 1) * 512],
                            psc[j % 2][:, :], AF.Square, scale=QSCALE_A))
                        sq_done[j] = k.cnt["ac"]
                    blk.scalar(_)
                    if j >= 4 * b:
                        def _(v, j=j):
                            mm = j - 4 * b
                            k.w(v, "v", "ac", sq_done[j])
                            k.i("dv", v.tensor_tensor(
                                at[:, (j % 2) * 512:(j % 2 + 1) * 512],
                                at[:, (j % 2) * 512:(j % 2 + 1) * 512],
                                am[:, mm * 512:(mm + 1) * 512], ALU.mult))
                            sq_done[j] = k.cnt["dv"]
                            sq_sem[j] = "dv"
                        blk.vector(_)

                def emit_oz(j):
                    def _(t, j=j):
                        k.w(t, "t", sq_sem[j], sq_done[j])
                        if j == 0:
                            prev = oz_evac.get(hp)
                            if prev:
                                k.w(t, "t", "dv", prev)
                        ins = t.matmul(
                            pso[:, :],
                            V_tm[:, j * 2048 + h * 128:j * 2048 + (h + 1) * 128],
                            at[:, (j % 2) * 512:(j % 2 + 1) * 512],
                            start=(j == 0), stop=(j == J - 1))
                        ins = t.matmul(
                            psz[:, :], ones_bf[:, 0:128],
                            at[:, (j % 2) * 512:(j % 2 + 1) * 512],
                            start=(j == 0), stop=False)
                        if j == J - 1:
                            ins = t.matmul(psz[:, :], ones_bf[0:1, 0:128],
                                           eps_bf[0:1, :],
                                           start=False, stop=True)
                        k.i("pe", ins)
                        oz_pe[j] = k.cnt["pe"]
                    blk.tensor(_)

                emit_sc(0)
                emit_sq(0)
                if J > 1:
                    emit_sc(1)
                    emit_sq(1)
                for j in range(J):
                    emit_oz(j)
                    if j + 2 < J:
                        emit_sc(j + 2)
                        emit_sq(j + 2)
                oz_t = k.cnt["pe"]
                qh_buf[qp].mark("pe", oz_t)

                def _(v, h=h, oz_t=oz_t):
                    k.w(v, "v", "pe", oz_t)
                    k.i("dv", v.reciprocal(zi[:, :], psz[:, :]))
                    k.w(v, "v", "dv")
                    if h == 0:
                        ofm_buf.wait_free(k, v, "v")
                    k.i("dv", v.tensor_tensor(ofm[:, h * 512:(h + 1) * 512],
                                              pso[:, :], zi[:, :], ALU.mult))
                blk.vector(_)
                oz_evac[hp] = k.cnt["dv"]

            def _(sy, b=b):
                k.w(sy, "s", "dv")
                k.i("oS", sy.dma_start(out=dram["Oa"][b], in_=ofm[:, :]), 16)
            blk.sync(_)
            ofm_buf.mark("oS", k.cnt["oS"])
        drain_dmas(k, blk)



def phase_outproj(nc, k, g, dram, name, osp_name, wname, x_in_name, x_out,
                  with_h=None):
    with ExitStack() as ctx:
        E = ctx.enter_context
        ofb = E(nc.sbuf_tensor(name + "_ofb", [128, 8192], BF16))
        cb0 = E(nc.sbuf_tensor(name + "_cb0", [128, 8192], BF16))
        cb1 = E(nc.sbuf_tensor(name + "_cb1", [128, 8192], BF16))
        xt = E(nc.sbuf_tensor(name + "_xt", [128, 8192], F32))
        ht = E(nc.sbuf_tensor(name + "_ht", [128, 8192], F32)) if with_h else None
        x1 = E(nc.sbuf_tensor(name + "_x1", [128, 8192], F32))
        po0 = E(nc.psum_tensor(name + "_p0", [128, 512], F32))
        po1 = E(nc.psum_tensor(name + "_p1", [128, 512], F32))
        blk = E(nc.Block(name))

        wsrc = dram[wname].rearrange("(dc p) f -> p dc f", p=128)
        cbbuf = [Buf(), Buf()]
        ofb_buf = Buf()
        xt_buf = Buf()
        x1_buf = Buf()
        po = (po0, po1)
        po_evac = {}
        x_in = dram[x_in_name]

        for b in range(4):
            def _(sa, b=b):
                ofb_buf.wait_free(k, sa, "a")
                k.i("oL", sa.dma_start(out=ofb[:, :], in_=dram[osp_name][b]), 16)
            blk.scalar(_)
            o_t = k.cnt["oL"]

            def _(sy, b=b):
                xt_buf.wait_free(k, sy, "s")
                for sc_i in range(4):
                    row = b * 512 + sc_i * 128
                    k.i("xL", sy.dma_start(
                        out=xt[:, sc_i * 2048:(sc_i + 1) * 2048],
                        in_=x_in[row:row + 128, :]), 16)
                if with_h:
                    for sc_i in range(4):
                        row = b * 512 + sc_i * 128
                        k.i("xL", sy.dma_start(
                            out=ht[:, sc_i * 2048:(sc_i + 1) * 2048],
                            in_=dram[with_h][row:row + 128, :]), 16)
            blk.sync(_)
            x_t = k.cnt["xL"]

            for j in range(4):
                cbp = j % 2
                cb = (cb0, cb1)[cbp]
                sem, tgt = load_cb_raw(k, blk,
                                       wsrc[:, :, j * 512:(j + 1) * 512],
                                       cb, cbbuf[cbp], WSEMP[cbp])
                for sc_i in range(4):
                    pp = po[(j * 4 + sc_i) % 2]

                    def _(t, j=j, sc_i=sc_i, pp=pp, cb=cb, sem=sem, tgt=tgt,
                          o_t=o_t):
                        k.w(t, "t", sem, tgt)
                        k.w(t, "t", "oL", o_t)
                        pv_ = po_evac.get((j * 4 + sc_i) % 2)
                        if pv_:
                            k.w(t, "t", "dv", pv_)
                        for dc in range(NDC):
                            ins = t.matmul(
                                pp[:, :],
                                ofb[:, dc * 512 + sc_i * 128:
                                    dc * 512 + (sc_i + 1) * 128],
                                cb[:, dc * 512:(dc + 1) * 512],
                                start=(dc == 0), stop=(dc == NDC - 1))
                        k.i("pe", ins)
                    blk.tensor(_)
                    mm_t = k.cnt["pe"]

                    def _(v, j=j, sc_i=sc_i, pp=pp, mm_t=mm_t, x_t=x_t):
                        k.w(v, "v", "pe", mm_t)
                        k.w(v, "v", "xL", x_t)
                        if j == 0 and sc_i == 0:
                            x1_buf.wait_free(k, v, "v")
                        dst = x1[:, sc_i * 2048 + j * 512:sc_i * 2048 + (j + 1) * 512]
                        if with_h:
                            k.i("dv", v.tensor_tensor(
                                dst, pp[:, :],
                                ht[:, sc_i * 2048 + j * 512:
                                   sc_i * 2048 + (j + 1) * 512], ALU.add))
                            k.w(v, "v", "dv")
                            k.i("dv", v.tensor_tensor(
                                dst, dst,
                                xt[:, sc_i * 2048 + j * 512:
                                   sc_i * 2048 + (j + 1) * 512], ALU.add))
                        else:
                            k.i("dv", v.tensor_tensor(
                                dst, pp[:, :],
                                xt[:, sc_i * 2048 + j * 512:
                                   sc_i * 2048 + (j + 1) * 512], ALU.add))
                    blk.vector(_)
                    po_evac[(j * 4 + sc_i) % 2] = k.cnt["dv"]
                cbbuf[cbp].mark("pe", k.cnt["pe"])
            ev_t = k.cnt["dv"]
            xt_buf.mark("dv", ev_t)
            ofb_buf.mark("pe", k.cnt["pe"])

            def _(sy, b=b, ev_t=ev_t):
                k.w(sy, "s", "dv", ev_t)
                for sc_i in range(4):
                    row = b * 512 + sc_i * 128
                    k.i("stX", sy.dma_start(
                        out=x_out[row:row + 128, :],
                        in_=x1[:, sc_i * 2048:(sc_i + 1) * 2048]), 16)
            blk.sync(_)
            x1_buf.mark("stX", k.cnt["stX"])
        drain_dmas(k, blk)



def phase_FFN(nc, k, g, dram, cache_done, name, x_in_name, x_out, call_idx):
    with ExitStack() as ctx:
        E = ctx.enter_context
        x_acc = E(nc.sbuf_tensor(name + "_xa", [128, 8192], F32))
        h_fm = E(nc.sbuf_tensor(name + "_hf", [128, 8192], BF16))
        mid = E(nc.sbuf_tensor(name + "_mid", [128, 8192], BF16))
        cbs = [E(nc.sbuf_tensor(name + f"_cb{i}", [128, 8192], BF16))
               for i in range(4)]
        h_tm = E(nc.sbuf_tensor(name + "_htm", [128, 2048], BF16))
        sbf = E(nc.sbuf_tensor(name + "_sbf", [128, 512], BF16))
        m1 = E(nc.sbuf_tensor(name + "_m1", [128, 512], BF16))
        ga_sb = E(nc.sbuf_tensor(name + "_ga", [128, 2048], BF16))
        ua_sb = E(nc.sbuf_tensor(name + "_ua", [128, 2048], BF16))
        da_sb = E(nc.sbuf_tensor(name + "_da", [128, 2048], BF16))
        gb_sb = E(nc.sbuf_tensor(name + "_gb", [64, 4096], BF16))
        ub_sb = E(nc.sbuf_tensor(name + "_ub", [64, 4096], BF16))
        db_sb = E(nc.sbuf_tensor(name + "_db", [64, 4096], BF16))
        ya_g = E(nc.sbuf_tensor(name + "_yag", [64, 512], BF16))
        ya_u = E(nc.sbuf_tensor(name + "_yau", [64, 512], BF16))
        za = E(nc.sbuf_tensor(name + "_za", [64, 512], BF16))
        wr_bf = E(nc.sbuf_tensor(name + "_wr", [128, 128], BF16))
        sel_sb = E(nc.sbuf_tensor(name + "_sel", [8, 1024], F32))
        id32 = E(nc.sbuf_tensor(name + "_id", [128, 128], F32))
        on32 = E(nc.sbuf_tensor(name + "_on", [128, 128], F32))
        w_tm = E(nc.sbuf_tensor(name + "_wtm", [128, 32], F32))
        w_fm = E(nc.sbuf_tensor(name + "_wfm", [8, 512], F32))
        rsc = E(nc.sbuf_tensor(name + "_rsc", [128, 128], F32))
        rs1 = E(nc.sbuf_tensor(name + "_rs1", [128, 8], F32))
        tmp_sb = E(nc.sbuf_tensor(name + "_tmp", [1, 1], F32))
        tinv = E(nc.sbuf_tensor(name + "_ti", [128, 1], F32))
        stats = E(nc.sbuf_tensor(name + "_st", [128, 24], F32))
        mv = E(nc.sbuf_tensor(name + "_mv", [128, 2], F32))
        vv = E(nc.sbuf_tensor(name + "_vv", [128, 1], F32))
        rstd = E(nc.sbuf_tensor(name + "_rsd", [128, 1], F32))
        pw = E(nc.psum_tensor(name + "_pw", [128, 512], F32))
        pg0 = E(nc.psum_tensor(name + "_pg0", [128, 512], F32))
        pg1 = E(nc.psum_tensor(name + "_pg1", [128, 512], F32))
        pu0 = E(nc.psum_tensor(name + "_pu0", [128, 512], F32))
        pu1 = E(nc.psum_tensor(name + "_pu1", [128, 512], F32))
        pd0 = E(nc.psum_tensor(name + "_pd0", [128, 512], F32))
        pd1 = E(nc.psum_tensor(name + "_pd1", [128, 512], F32))
        blk = E(nc.Block(name))

        scr = {"stats": stats, "mv": mv, "vv": vv, "rstd": rstd}
        ident_bf = g["ident_bf"]
        x_in = dram[x_in_name]
        cbbuf = [Buf() for _ in range(4)]
        htm_buf = Buf()
        xacc_buf = Buf()
        pg = (pg0, pg1)
        pu = (pu0, pu1)
        pd = (pd0, pd1)
        tp_ps = [pd0[:, :].bitcast(BF16), pd1[:, :].bitcast(BF16)]
        wgsrc = dram["wg"].rearrange("e (dc p) f -> e p dc f", p=128)
        wusrc = dram["wu"].rearrange("e (dc p) f -> e p dc f", p=128)
        wdsrc = dram["wd"].rearrange("e (fc p) d -> e p fc d", p=128)

        def _(gp):
            k.i("cst", gp.dma_start(out=wr_bf[:, :], in_=dram["wrp"]), 16)
            k.i("cst", gp.dma_start(out=id32[:, :], in_=dram["identc"]), 16)
            k.i("cst", gp.dma_start(out=on32[:, :], in_=dram["onesc"]), 16)
            k.i("cst", gp.dma_start(out=sel_sb[:, :], in_=dram["selc"]), 16)
            for nm_, dst in (("ga", ga_sb), ("ua", ua_sb), ("da", da_sb)):
                src = dram[nm_].rearrange("l (dc p) r -> p l dc r", p=128)
                k.i("cst", gp.dma_start(
                    out=dst[:, :].rearrange("p (l dc r) -> p l dc r", l=2, r=RLO),
                    in_=src), 16)
            for nm_, dst in (("gb", gb_sb), ("ub", ub_sb), ("db", db_sb)):
                src = dram[nm_].rearrange("l r f -> r l f")
                k.i("cst", gp.dma_start(
                    out=dst[:, :].rearrange("r (l f) -> r l f", l=2), in_=src), 16)
            k.i("cst", gp.dma_start(out=tmp_sb[:, :], in_=dram["temp"]), 16)
        blk.gpsimd(_)
        smalls_t = k.cnt["cst"]

        def _(v):
            k.w(v, "v", "cst", smalls_t)
            k.i("dv", v.reciprocal(tmp_sb[:, :], tmp_sb[:, :]))
        blk.vector(_)

        def _(t):
            k.w(t, "t", "dv")
            k.i("pe", t.matmul(pw[:, 0:1], on32[0:1, 0:128], tmp_sb[0:1, 0:1],
                               start=True, stop=True))
        blk.tensor(_)

        def _(v):
            k.w(v, "v", "pe")
            k.i("dv", v.tensor_copy(tinv[:, :], pw[:, 0:1]))
        blk.vector(_)
        tinv_t = k.cnt["dv"]
        prev_expert_evac = 0

        def load_cb(kind, e, cbg, bi, queue):
            tag = (kind, e, cbg)
            cb = cbs[bi]
            buf = cbbuf[bi]
            wsem = WSEM[bi]
            if tag not in cache_done:
                src = {"wg": wgsrc, "wu": wusrc, "wd": wdsrc}[kind]
                s_, t_ = load_cb_raw(k, blk, src[e, :, :, cbg * 512:(cbg + 1) * 512],
                                     cb, buf, WSEMP[bi])

                def _(sy, s_=s_, t_=t_, kind=kind, e=e, cbg=cbg, cb=cb):
                    k.w(sy, "s", s_, t_)
                    k.i(CWSEM[bi], sy.dma_start(
                        out=dram["c" + kind][e][cbg], in_=cb[:, :]), 16)
                blk.sync(_)
                cache_done[tag] = True
                buf.mark(CWSEM[bi], k.cnt[CWSEM[bi]])
                return (s_, t_)
            ek = "s" if queue == "s" else "a"

            def _(eng, ek=ek):
                for cwn in CWSEM.values():
                    k.w(eng, ek, cwn)
                buf.wait_free(k, eng, ek)
                k.i(wsem, eng.dma_start(out=cb[:, :],
                                        in_=dram["c" + kind][e][cbg]), 16)
            (blk.sync if queue == "s" else blk.scalar)(_)
            return (wsem, k.cnt[wsem])

        for tt in range(4):
            trow = tt * 512
            tp_state = {}
            def _(sy, trow=trow):
                xacc_buf.wait_free(k, sy, "s")
                for tc in range(4):
                    row0 = trow + tc * 128
                    k.i("xL", sy.dma_start(
                        out=x_acc[:, tc * 2048:(tc + 1) * 2048],
                        in_=x_in[row0:row0 + 128, :]), 16)
            blk.sync(_)
            x_t = k.cnt["xL"]

            for tc in range(4):
                def _(v, x_t=x_t):
                    k.w(v, "v", "xL", x_t)
                    htm_buf.wait_free(k, v, "v")
                blk.vector(_)
                ln_t = emit_ln_tile(k, blk, x_acc[:, tc * 2048:(tc + 1) * 2048],
                                    h_tm[:, :], scr)

                def hw(dc, tc=tc):
                    return h_fm[:, dc * 512 + tc * 128:dc * 512 + (tc + 1) * 128]

                def _(t, ln_t=ln_t, pev=prev_expert_evac):
                    k.w(t, "t", "dv", ln_t)
                    k.w(t, "t", "dv", pev)
                blk.tensor(_)
                emit_transpose_tile(k, blk, h_tm, hw, tp_ps, tp_state, ident_bf)
                htm_buf.mark("pe", k.cnt["pe"])
            hfm_ready = k.cnt["dv"]

            inh_t = [0] * 4
            for tc in range(4):
                def _(t, tc=tc):
                    k.w(t, "t", "dv", hfm_ready)
                    k.w(t, "t", "cst", smalls_t)
                    if tc > 0:
                        k.w(t, "t", "ac", inh_t[tc - 1])
                    for dc in range(NDC):
                        ins = t.matmul(
                            pd0[:, 0:8],
                            h_fm[:, dc * 512 + tc * 128:dc * 512 + (tc + 1) * 128],
                            wr_bf[:, dc * 8:(dc + 1) * 8],
                            start=(dc == 0), stop=(dc == NDC - 1))
                    k.i("pe", ins)
                blk.tensor(_)
                lg_t = k.cnt["pe"]

                def _(sc, tc=tc, lg_t=lg_t):
                    k.w(sc, "a", "pe", lg_t)
                    k.w(sc, "a", "dv", tinv_t)
                    k.i("ac", sc.activation(rsc[:, tc * 8:(tc + 1) * 8],
                                            pd0[:, 0:8], AF.Sigmoid,
                                            scale=tinv[:, 0:1]))
                blk.scalar(_)
                inh_t[tc] = k.cnt["ac"]

            def _(v):
                k.w(v, "v", "ac", inh_t[3])
                for tc in range(4):
                    k.i("dv", v.tensor_scalar(rsc[:, 32 + tc * 8:32 + (tc + 1) * 8],
                                              rsc[:, tc * 8:(tc + 1) * 8],
                                              GOLD_C, None, ALU.subtract))
            blk.vector(_)

            def _(sc):
                k.w(sc, "a", "dv")
                for tc in range(4):
                    k.i("ac", sc.activation(rsc[:, 32 + tc * 8:32 + (tc + 1) * 8],
                                            rsc[:, 32 + tc * 8:32 + (tc + 1) * 8],
                                            AF.Abs))
                k.w(sc, "a", "ac")
                for tc in range(4):
                    k.i("ac", sc.activation(rsc[:, 64 + tc * 8:64 + (tc + 1) * 8],
                                            rsc[:, 32 + tc * 8:32 + (tc + 1) * 8],
                                            AF.Exp, scale=-10.0))
                for tc in range(4):
                    k.i("ac", sc.activation(rsc[:, 96 + tc * 8:96 + (tc + 1) * 8],
                                            rsc[:, 32 + tc * 8:32 + (tc + 1) * 8],
                                            AF.Exp, scale=-(1.0 / 0.3)))
            blk.scalar(_)
            acts_t = k.cnt["ac"]

            for tc in range(4):
                _emit_router_tc(k, blk, rsc, rs1, w_tm, tc, acts_t)
            wt_t = k.cnt["dv"]
            router_ac = k.cnt["ac"]

            for tc in range(4):
                def _(t, tc=tc, wt_t=wt_t):
                    k.w(t, "t", "dv", wt_t)
                    k.i("pe", t.transpose(pd1[0:8, 0:128],
                                          w_tm[:, tc * 8:(tc + 1) * 8],
                                          id32[:, :]))
                blk.tensor(_)
                t_t = k.cnt["pe"]

                def _(v, tc=tc, t_t=t_t):
                    k.w(v, "v", "pe", t_t)
                    k.i("dv", v.tensor_copy(w_fm[:, tc * 128:(tc + 1) * 128],
                                            pd1[0:8, 0:128]))
                blk.vector(_)

                def _(t, tgt=k.cnt["dv"]):
                    k.w(t, "t", "dv", tgt)
                blk.tensor(_)
            wfm_t = k.cnt["dv"]

            pend = {}
            gu_evac = {}
            pd_evac = {}

            def preload_pair(e, cbg):
                p2 = cbg % 2
                sA = load_cb("wg", e, cbg, 2 * p2, "s")
                sB = load_cb("wu", e, cbg, 2 * p2 + 1, "a")
                pend[(e, "gu", cbg)] = (sA, sB, p2)

            def preload_wd(e, j):
                bi = j % 4
                sD = load_cb("wd", e, j, bi, "s" if j % 2 == 0 else "a")
                pend[(e, "d", j)] = (sD, bi)

            preload_pair(0, 0)
            for e in range(NE):
                is_lo = e < 2

                def _(t, e=e, pev=prev_expert_evac):
                    k.w(t, "t", "dv", wfm_t)
                    k.w(t, "t", "dv", pev)
                    k.i("pe", t.matmul(pw[:, :], sel_sb[:, e * 128:(e + 1) * 128],
                                       w_fm[:, :], start=True, stop=True))
                blk.tensor(_)
                pw_t = k.cnt["pe"]

                ya_done = 0
                if is_lo:
                    for asb, ysb, pp in ((ga_sb, ya_g, pg0), (ua_sb, ya_u, pu0)):
                        def _(t, e=e, asb=asb, pp=pp, pev=prev_expert_evac):
                            k.w(t, "t", "cst", smalls_t)
                            k.w(t, "t", "dv", hfm_ready)
                            k.w(t, "t", "dv", pev)
                            for nm_, v_ in gu_evac.get(0, ()):
                                k.w(t, "t", nm_, v_)
                            k.w(t, "t", "ac", router_ac)
                            for dc in range(NDC):
                                ins = t.matmul(
                                    pp[0:64, :],
                                    asb[:, e * 1024 + dc * RLO:
                                        e * 1024 + (dc + 1) * RLO],
                                    h_fm[:, dc * 512:(dc + 1) * 512],
                                    start=(dc == 0), stop=(dc == NDC - 1))
                            k.i("pe", ins)
                        blk.tensor(_)
                        ya_t = k.cnt["pe"]

                        def _(v, ysb=ysb, pp=pp, ya_t=ya_t):
                            k.w(v, "v", "pe", ya_t)
                            k.i("dv", v.tensor_copy(ysb[:, :], pp[0:64, :]))
                        blk.vector(_)

                        def _(t, tgt=k.cnt["dv"]):
                            k.w(t, "t", "dv", tgt)
                        blk.tensor(_)
                    ya_done = k.cnt["dv"]

                for cbg in range(4):
                    sA, sB, p2 = pend.pop((e, "gu", cbg))
                    if cbg + 1 < 4:
                        preload_pair(e, cbg + 1)
                    else:
                        preload_wd(e, 0)
                    for fi in range(4):
                        fc = cbg * 4 + fi
                        _emit_gateup(k, blk, e, fc, fi, cbs[2 * p2],
                                     cbs[2 * p2 + 1], sA, sB, h_fm, hfm_ready,
                                     pg, pu, pw, pw_t, gb_sb, ub_sb, ya_g,
                                     ya_u, ya_done, sbf, m1, mid, is_lo,
                                     prev_expert_evac, router_ac, gu_evac)
                    cbbuf[2 * p2].mark("pe", k.cnt["pe"])
                    cbbuf[2 * p2 + 1].mark("pe", k.cnt["pe"])
                mid_ready = k.cnt["dv"]

                za_done = 0
                if is_lo:
                    def _(t, e=e, mid_ready=mid_ready):
                        k.w(t, "t", "dv", mid_ready)
                        k.w(t, "t", "ac", router_ac)
                        pv_ = pd_evac.get(0)
                        if pv_:
                            k.w(t, "t", "dv", pv_)
                        for fc in range(NDC):
                            ins = t.matmul(
                                pd0[0:64, :],
                                da_sb[:, e * 1024 + fc * RLO:
                                      e * 1024 + (fc + 1) * RLO],
                                mid[:, fc * 512:(fc + 1) * 512],
                                start=(fc == 0), stop=(fc == NDC - 1))
                        k.i("pe", ins)
                    blk.tensor(_)
                    za_t = k.cnt["pe"]

                    def _(v, za_t=za_t):
                        k.w(v, "v", "pe", za_t)
                        k.i("dv", v.tensor_copy(za[:, :], pd0[0:64, :]))
                    blk.vector(_)

                    def _(t, tgt=k.cnt["dv"]):
                        k.w(t, "t", "dv", tgt)
                    blk.tensor(_)
                    za_done = k.cnt["dv"]

                for j in range(4):
                    sD, bi = pend.pop((e, "d", j))
                    if j + 1 < 4:
                        preload_wd(e, j + 1)
                    elif e + 1 < NE:
                        preload_pair(e + 1, 0)
                    for tc in range(4):
                        pdx = pd[(j * 4 + tc) % 2]

                        def _(t, e=e, j=j, tc=tc, pdx=pdx, sD=sD, bi=bi,
                              mid_ready=mid_ready, za_done=za_done):
                            k.w(t, "t", sD[0], sD[1])
                            k.w(t, "t", "dv", mid_ready)
                            k.w(t, "t", "ac", router_ac)
                            pv_ = pd_evac.get((j * 4 + tc) % 2)
                            if pv_:
                                k.w(t, "t", "dv", pv_)
                            if e < 2:
                                k.w(t, "t", "dv", za_done)
                            for fc in range(NDC):
                                last = (fc == NDC - 1) and (e >= 2)
                                ins = t.matmul(
                                    pdx[:, :],
                                    mid[:, fc * 512 + tc * 128:
                                        fc * 512 + (tc + 1) * 128],
                                    cbs[bi][:, fc * 512:(fc + 1) * 512],
                                    start=(fc == 0), stop=last)
                            if e < 2:
                                ins = t.matmul(
                                    pdx[:, :], za[:, tc * 128:(tc + 1) * 128],
                                    db_sb[:, e * 2048 + j * 512:
                                          e * 2048 + (j + 1) * 512],
                                    start=False, stop=True)
                            k.i("pe", ins)
                        blk.tensor(_)
                        dm_t = k.cnt["pe"]

                        def _(v, j=j, tc=tc, pdx=pdx, dm_t=dm_t):
                            k.w(v, "v", "pe", dm_t)
                            dst = x_acc[:, tc * 2048 + j * 512:
                                        tc * 2048 + (j + 1) * 512]
                            k.i("dv", v.tensor_tensor(dst, dst, pdx[:, :],
                                                      ALU.add))
                        blk.vector(_)
                        pd_evac[(j * 4 + tc) % 2] = k.cnt["dv"]
                    cbbuf[bi].mark("pe", k.cnt["pe"])
                prev_expert_evac = k.cnt["dv"]

            acc_t = k.cnt["dv"]

            def _(sy, trow=trow, acc_t=acc_t):
                k.w(sy, "s", "dv", acc_t)
                for tc in range(4):
                    k.i("stX", sy.dma_start(
                        out=x_out[trow + tc * 128:trow + (tc + 1) * 128, :],
                        in_=x_acc[:, tc * 2048:(tc + 1) * 2048]), 16)
            blk.sync(_)
            xacc_buf.mark("stX", k.cnt["stX"])
        drain_dmas(k, blk)


def _emit_router_tc(k, blk, rsc, rs1, w_tm, tc, acts_t):
    inh = rsc[:, tc * 8:(tc + 1) * 8]
    dist = rsc[:, 32 + tc * 8:32 + (tc + 1) * 8]
    wexp = rsc[:, 64 + tc * 8:64 + (tc + 1) * 8]
    fbt = rsc[:, 96 + tc * 8:96 + (tc + 1) * 8]

    def _(v):
        k.w(v, "v", "ac", acts_t)
        k.w(v, "v", "dv")
        k.i("dv", v.tensor_scalar(rs1[:, 0:8], inh, GOLD_L, None, ALU.is_ge))
        k.w(v, "v", "dv")
        k.i("dv", v.tensor_scalar(dist, inh, GOLD_U, None, ALU.is_le))
        k.w(v, "v", "dv")
        k.i("dv", v.tensor_tensor(rs1[:, 0:8], rs1[:, 0:8], dist, ALU.mult))
        k.w(v, "v", "dv")
        k.i("dv", v.tensor_tensor(wexp, wexp, rs1[:, 0:8], ALU.mult))
        k.w(v, "v", "dv")
        k.i("dv", v.tensor_reduce(rs1[:, 0:1], wexp, AX.X, ALU.add))
        k.w(v, "v", "dv")
        k.i("dv", v.tensor_scalar(rs1[:, 1:2], rs1[:, 0:1], 1e-8, None,
                                  ALU.is_lt))
        k.i("dv", v.tensor_reduce(rs1[:, 2:3], fbt, AX.X, ALU.max))
        k.w(v, "v", "dv")
        k.i("dv", v.tensor_scalar(dist, fbt, rs1[:, 2:3], None, ALU.is_equal))
        k.w(v, "v", "dv")
        k.i("dv", v.scalar_tensor_tensor(inh, dist, NEG_BIG, fbt, ALU.mult,
                                         ALU.add))
        k.w(v, "v", "dv")
        k.i("dv", v.tensor_reduce(rs1[:, 3:4], inh, AX.X, ALU.max))
        k.w(v, "v", "dv")
        k.i("dv", v.tensor_scalar(inh, fbt, rs1[:, 3:4], None, ALU.is_equal))
        k.w(v, "v", "dv")
        k.i("dv", v.tensor_tensor(dist, dist, inh, ALU.add))
        k.w(v, "v", "dv")
        k.i("dv", v.tensor_tensor(fbt, fbt, dist, ALU.mult))
        k.w(v, "v", "dv")
        k.i("dv", v.tensor_reduce(rs1[:, 4:5], fbt, AX.X, ALU.add))
        k.w(v, "v", "dv")
        k.i("dv", v.tensor_scalar(rs1[:, 4:5], rs1[:, 4:5], 1e-8, None, ALU.max))
        k.w(v, "v", "dv")
        k.i("dv", v.reciprocal(rs1[:, 4:5], rs1[:, 4:5]))
        k.w(v, "v", "dv")
        k.i("dv", v.tensor_scalar(fbt, fbt, rs1[:, 4:5], None, ALU.mult))
        k.w(v, "v", "dv")
        k.i("dv", v.tensor_tensor(dist, fbt, wexp, ALU.subtract))
        k.w(v, "v", "dv")
        k.i("dv", v.scalar_tensor_tensor(wexp, dist, rs1[:, 1:2], wexp,
                                         ALU.mult, ALU.add))
        k.w(v, "v", "dv")
        k.i("dv", v.tensor_reduce(rs1[:, 5:6], wexp, AX.X, ALU.add))
        k.w(v, "v", "dv")
        k.i("dv", v.tensor_scalar(rs1[:, 5:6], rs1[:, 5:6], 1e-8, None, ALU.max))
        k.w(v, "v", "dv")
        k.i("dv", v.reciprocal(rs1[:, 5:6], rs1[:, 5:6]))
        k.w(v, "v", "dv")
        k.i("dv", v.tensor_scalar(w_tm[:, tc * 8:(tc + 1) * 8], wexp,
                                  rs1[:, 5:6], None, ALU.mult))
    blk.vector(_)


def _emit_gateup(k, blk, e, fc, fi, cbG, cbU, sA, sB, h_fm, hfm_ready, pg, pu,
                 pw, pw_t, gb_sb, ub_sb, ya_g, ya_u, ya_done, sbf, m1, mid,
                 is_lo, prev_expert_evac, router_ac, gu_evac):
    par = fc % 2

    def _(t):
        k.w(t, "t", sA[0], sA[1])
        k.w(t, "t", sB[0], sB[1])
        k.w(t, "t", "dv", hfm_ready)
        k.w(t, "t", "dv", prev_expert_evac)
        k.w(t, "t", "ac", router_ac)
        for nm_, v_ in gu_evac.get(par, ()):
            k.w(t, "t", nm_, v_)
        if is_lo:
            k.w(t, "t", "dv", ya_done)
        for dc in range(NDC):
            ins = t.matmul(pg[par][:, :],
                           cbG[:, dc * 512 + fi * 128:dc * 512 + (fi + 1) * 128],
                           h_fm[:, dc * 512:(dc + 1) * 512],
                           start=(dc == 0),
                           stop=(not is_lo and dc == NDC - 1))
        if is_lo:
            ins = t.matmul(pg[par][:, :],
                           gb_sb[:, e * 2048 + fc * 128:e * 2048 + (fc + 1) * 128],
                           ya_g[:, :], start=False, stop=True)
        for dc in range(NDC):
            ins = t.matmul(pu[par][:, :],
                           cbU[:, dc * 512 + fi * 128:dc * 512 + (fi + 1) * 128],
                           h_fm[:, dc * 512:(dc + 1) * 512],
                           start=(dc == 0),
                           stop=(not is_lo and dc == NDC - 1))
        if is_lo:
            ins = t.matmul(pu[par][:, :],
                           ub_sb[:, e * 2048 + fc * 128:e * 2048 + (fc + 1) * 128],
                           ya_u[:, :], start=False, stop=True)
        k.i("pe", ins)
    blk.tensor(_)
    mm_t = k.cnt["pe"]

    def _(sc):
        k.w(sc, "a", "pe", mm_t)
        k.w(sc, "a", "dv")
        k.i("ac", sc.activation(sbf[:, :], pg[par][:, :], AF.Silu))
    blk.scalar(_)
    s_t = k.cnt["ac"]

    def _(v):
        k.w(v, "v", "ac", s_t)
        k.w(v, "v", "pe", pw_t)
        k.i("dv", v.tensor_tensor(m1[:, :], sbf[:, :], pu[par][:, :], ALU.mult))
        k.w(v, "v", "dv")
        k.i("dv", v.tensor_tensor(mid[:, fc * 512:(fc + 1) * 512], m1[:, :],
                                  pw[:, :], ALU.mult))
    blk.vector(_)
    gu_evac[par] = (("ac", s_t), ("dv", k.cnt["dv"]))



def phase_B1(nc, k, g, dram):
    with ExitStack() as ctx:
        E = ctx.enter_context
        xb = E(nc.sbuf_tensor("b1_xb", [128, 2048], F32))
        hf32 = E(nc.sbuf_tensor("b1_h", [128, 2048], F32))
        h2 = E(nc.sbuf_tensor("b1_h2", [128, 2048], BF16))
        xn_fm = E(nc.sbuf_tensor("b1_xn", [128, 8192], BF16))
        cb0 = E(nc.sbuf_tensor("b1_cb0", [128, 8192], BF16))
        cb1 = E(nc.sbuf_tensor("b1_cb1", [128, 8192], BF16))
        qo = E(nc.sbuf_tensor("b1_qo", [128, 512], BF16))
        stats = E(nc.sbuf_tensor("b1_st", [128, 24], F32))
        mv = E(nc.sbuf_tensor("b1_mv", [128, 2], F32))
        vv = E(nc.sbuf_tensor("b1_vv", [128, 1], F32))
        rstd = E(nc.sbuf_tensor("b1_rs", [128, 1], F32))
        tp0 = E(nc.psum_tensor("b1_tp0", [128, 1024], BF16))
        tp1 = E(nc.psum_tensor("b1_tp1", [128, 1024], BF16))
        pqa = E(nc.psum_tensor("b1_pqa", [128, 512], F32))
        pqb = E(nc.psum_tensor("b1_pqb", [128, 512], F32))
        pv = E(nc.psum_tensor("b1_pv", [128, 512], F32))
        blk = E(nc.Block("B1"))

        scr = {"stats": stats, "mv": mv, "vv": vv, "rstd": rstd}
        K_fm, V_tm = g["K_fm"], g["V_tm"]
        ident_bf = g["ident_bf"]
        gbg, gbb = g["gbg"], g["gbb"]
        qsrc = dram["qkv"].rearrange("(dc p) f -> p dc f", p=128)
        pqs = (pqa, pqb)
        pq_evac = {}
        pv_evac = [0]
        cbbuf = [Buf(), Buf()]
        xb_buf = Buf()
        h_buf = Buf()
        h2_buf = Buf()
        qo_buf = Buf()
        tp_state = {}

        for s in range(4):
            scol = s * 512
            for ti in range(4):
                row0 = scol + ti * 128

                def _(sy, row0=row0):
                    xb_buf.wait_free(k, sy, "s")
                    k.i("xL", sy.dma_start(out=xb[:, :],
                                           in_=dram["XB"][row0:row0 + 128, :]), 16)
                blk.sync(_)
                xld = k.cnt["xL"]

                def _(v, xld=xld):
                    k.w(v, "v", "xL", xld)
                    h_buf.wait_free(k, v, "v")
                blk.vector(_)
                emit_ln_tile(k, blk, xb[:, :], hf32[:, :], scr)
                xb_buf.mark("dv", k.cnt["dv"])
                h_t = k.cnt["dv"]

                def _(sy, row0=row0, h_t=h_t):
                    k.w(sy, "s", "dv", h_t)
                    k.i("hS", sy.dma_start(out=dram["Hb"][row0:row0 + 128, :],
                                           in_=hf32[:, :]), 16)
                blk.sync(_)
                h_buf.mark("hS", k.cnt["hS"])

                def _(v):
                    h2_buf.wait_free(k, v, "v")
                blk.vector(_)
                ln2_t = emit_ln_tile(k, blk, hf32[:, :], h2[:, :], scr)
                h_buf.mark("dv", ln2_t)

                def hw(dc, ti=ti):
                    return xn_fm[:, dc * 512 + ti * 128:dc * 512 + (ti + 1) * 128]

                def _(t, ln2_t=ln2_t):
                    k.w(t, "t", "dv", ln2_t)
                blk.tensor(_)
                emit_transpose_tile(k, blk, h2, hw, [tp0, tp1], tp_state,
                                    ident_bf,
                                    affine=(lambda dc: gbg[:, dc:dc + 1],
                                            lambda dc: gbb[:, dc:dc + 1]))
                h2_buf.mark("pe", k.cnt["pe"])
            xn_ready = k.cnt["ac"]

            for r, rname in ((1, "k"), (2, "v"), (0, "q")):
                for cbg in range(4):
                    cbp = (r * 4 + cbg) % 2
                    cb = (cb0, cb1)[cbp]
                    sem, tgt = load_cb_raw(
                        k, blk,
                        qsrc[:, :, r * 2048 + cbg * 512:r * 2048 + (cbg + 1) * 512],
                        cb, cbbuf[cbp], WSEMP[cbp])
                    if rname in ("k", "q"):
                        for fi in range(4):
                            fc = cbg * 4 + fi
                            par = fc % 2
                            pq = pqs[par]

                            def _(t, fc=fc, fi=fi, cb=cb, sem=sem, tgt=tgt,
                                  xn_ready=xn_ready, par=par, pq=pq):
                                k.w(t, "t", sem, tgt)
                                k.w(t, "t", "ac", xn_ready)
                                prev = pq_evac.get(par)
                                if prev:
                                    k.w(t, "t", "dv", prev)
                                for dc in range(NDC):
                                    ins = t.matmul(
                                        pq[:, :],
                                        cb[:, dc * 512 + fi * 128:
                                           dc * 512 + (fi + 1) * 128],
                                        xn_fm[:, dc * 512:(dc + 1) * 512],
                                        start=(dc == 0), stop=(dc == NDC - 1))
                                k.i("pe", ins)
                            blk.tensor(_)
                            mm_t = k.cnt["pe"]
                            if rname == "k":
                                def _(v, fc=fc, scol=scol, mm_t=mm_t, pq=pq):
                                    k.w(v, "v", "pe", mm_t)
                                    k.i("dv", v.tensor_copy(
                                        K_fm[:, fc * 2048 + scol:
                                             fc * 2048 + scol + 512], pq[:, :]))
                                blk.vector(_)
                                pq_evac[par] = k.cnt["dv"]
                            else:
                                def _(v, mm_t=mm_t, pq=pq):
                                    k.w(v, "v", "pe", mm_t)
                                    qo_buf.wait_free(k, v, "v")
                                    k.i("dv", v.tensor_copy(qo[:, :], pq[:, :]))
                                blk.vector(_)
                                q_t = k.cnt["dv"]
                                pq_evac[par] = q_t

                                def _(sy, fc=fc, scol=scol, q_t=q_t):
                                    k.w(sy, "s", "dv", q_t)
                                    k.i("qS", sy.dma_start(
                                        out=dram["Qb"][fc * 128:(fc + 1) * 128,
                                                       scol:scol + 512],
                                        in_=qo[:, :]), 16)
                                blk.sync(_)
                                qo_buf.mark("qS", k.cnt["qS"])
                    else:
                        for tc in range(4):
                            tcg = s * 4 + tc

                            def _(t, tc=tc, cb=cb, sem=sem, tgt=tgt,
                                  xn_ready=xn_ready):
                                k.w(t, "t", sem, tgt)
                                k.w(t, "t", "ac", xn_ready)
                                if pv_evac[0]:
                                    k.w(t, "t", "ac", pv_evac[0])
                                for dc in range(NDC):
                                    ins = t.matmul(
                                        pv[:, :],
                                        xn_fm[:, dc * 512 + tc * 128:
                                              dc * 512 + (tc + 1) * 128],
                                        cb[:, dc * 512:(dc + 1) * 512],
                                        start=(dc == 0), stop=(dc == NDC - 1))
                                k.i("pe", ins)
                            blk.tensor(_)
                            mm_t = k.cnt["pe"]

                            def _(sc, tcg=tcg, cbg=cbg, mm_t=mm_t):
                                k.w(sc, "a", "pe", mm_t)
                                k.i("ac", sc.activation(
                                    V_tm[:, tcg * 2048 + cbg * 512:
                                         tcg * 2048 + (cbg + 1) * 512],
                                    pv[:, :], AF.Identity))
                            blk.scalar(_)
                            pv_evac[0] = k.cnt["ac"]
                    cbbuf[cbp].mark("pe", k.cnt["pe"])
        drain_dmas(k, blk)



def phase_B2(nc, k, g, dram):
    with ExitStack() as ctx:
        E = ctx.enter_context
        qb = E(nc.sbuf_tensor("b2_qb", [128, 8192], BF16))
        p_sb = E(nc.sbuf_tensor("b2_p", [128, 2048], BF16))
        pT = E(nc.sbuf_tensor("b2_pt", [128, 2048], BF16))
        osb = E(nc.sbuf_tensor("b2_o", [128, 2048], BF16))
        bm = E(nc.sbuf_tensor("b2_bm", [128, 2048], F32))
        mx = E(nc.sbuf_tensor("b2_mx", [128, 1], F32))
        nb2 = E(nc.sbuf_tensor("b2_nb", [128, 1], F32))
        rs = E(nc.sbuf_tensor("b2_rs", [128, 1], F32))
        psS = E(nc.psum_tensor("b2_ps", [128, 2048], F32))
        pO0 = E(nc.psum_tensor("b2_po0", [128, 512], F32))
        pO1 = E(nc.psum_tensor("b2_po1", [128, 512], F32))
        ptp0 = E(nc.psum_tensor("b2_tp0", [128, 1024], BF16))
        ptp1 = E(nc.psum_tensor("b2_tp1", [128, 1024], BF16))
        blk = E(nc.Block("B2"))

        K_fm, V_tm = g["K_fm"], g["V_tm"]
        ident_bf = g["ident_bf"]
        qb_buf = Buf()
        o_buf = Buf()
        pO = (pO0, pO1)
        ptp = (ptp0, ptp1)

        def _(gp):
            for m in range(4):
                k.i("cst", gp.dma_start(out=bm[:, m * 512:(m + 1) * 512],
                                        in_=dram["bmc"][m]), 16)
        blk.gpsimd(_)
        bm_t = k.cnt["cst"]
        prev_exp = 0
        prev_tp_pe = 0

        for b in range(4):
            def _(sa, b=b):
                qb_buf.wait_free(k, sa, "a")
                for dc in range(NDC):
                    k.i("qL0", sa.dma_start(
                        out=qb[:, dc * 512:(dc + 1) * 512],
                        in_=dram["Qb"][dc * 128:(dc + 1) * 128,
                                       b * 512:(b + 1) * 512]), 16)
            blk.scalar(_)
            q_t = k.cnt["qL0"]

            for si in range(4):
                NB = b + 1
                Tc = NB * 512

                def _(t, b=b, si=si, q_t=q_t, prev_exp=prev_exp):
                    k.w(t, "t", "qL0", q_t)
                    k.w(t, "t", "cst", bm_t)
                    k.w(t, "t", "ac", prev_exp)
                    for tb in range(NB):
                        for dc in range(NDC):
                            ins = t.matmul(
                                psS[:, tb * 512:(tb + 1) * 512],
                                qb[:, dc * 512 + si * 128:dc * 512 + (si + 1) * 128],
                                K_fm[:, dc * 2048 + tb * 512:
                                     dc * 2048 + (tb + 1) * 512],
                                start=(dc == 0), stop=(dc == NDC - 1))
                    k.i("pe", ins)
                blk.tensor(_)
                scm_t = k.cnt["pe"]

                def _(v, b=b, si=si, scm_t=scm_t, Tc=Tc):
                    k.w(v, "v", "pe", scm_t)
                    k.w(v, "v", "ac")
                    k.i("dv", v.tensor_tensor(psS[:, b * 512:(b + 1) * 512],
                                              psS[:, b * 512:(b + 1) * 512],
                                              bm[:, si * 512:(si + 1) * 512],
                                              ALU.add))
                    k.w(v, "v", "dv")
                    k.i("dv", v.tensor_reduce(mx[:, :], psS[:, 0:Tc], AX.X,
                                              ALU.max))
                    k.w(v, "v", "dv")
                    k.i("dv", v.tensor_scalar(nb2[:, :], mx[:, :], -CSCALE_B,
                                              None, ALU.mult))
                blk.vector(_)
                mx_t = k.cnt["dv"]

                def _(sc, Tc=Tc, mx_t=mx_t, prev_tp_pe=prev_tp_pe):
                    k.w(sc, "a", "dv", mx_t)
                    k.w(sc, "a", "pe", prev_tp_pe)
                    k.i("ac", sc.activation(p_sb[:, 0:Tc], psS[:, 0:Tc], AF.Exp,
                                            scale=CSCALE_B, bias=nb2[:, 0:1],
                                            accum_out=rs[:, 0:1]))
                blk.scalar(_)
                exp_t = k.cnt["ac"]
                prev_exp = exp_t

                def _(v, Tc=Tc, exp_t=exp_t):
                    k.w(v, "v", "ac", exp_t)
                    k.i("dv", v.reciprocal(rs[:, :], rs[:, :]))
                    k.w(v, "v", "dv")
                    k.i("dv", v.tensor_scalar(p_sb[:, 0:Tc], p_sb[:, 0:Tc],
                                              rs[:, 0:1], None, ALU.mult))
                blk.vector(_)
                pn_t = k.cnt["dv"]

                ntiles = NB * 4
                tp_state = {}
                for j in range(ntiles):
                    bank = j % 2

                    def _(t, j=j, bank=bank, pn_t=pn_t):
                        k.w(t, "t", "dv", pn_t)
                        if bank in tp_state:
                            nm, v_ = tp_state[bank]
                            k.w(t, "t", nm, v_)
                        k.i("pe", t.transpose(
                            ptp[bank][:, 0:128],
                            p_sb[:, j * 128:(j + 1) * 128], ident_bf[:, :]))
                    blk.tensor(_)
                    t_t = k.cnt["pe"]

                    def _(v, j=j, bank=bank, t_t=t_t):
                        k.w(v, "v", "pe", t_t)
                        tgt = k.i("dv", v.tensor_copy(
                            pT[:, j * 128:(j + 1) * 128],
                            ptp[bank][:, 0:128]))
                        tp_state[bank] = ("dv", tgt)
                    blk.vector(_)
                pT_t = k.cnt["dv"]
                prev_tp_pe = k.cnt["pe"]

                for dc in range(NDC):
                    pp = pO[dc % 2]

                    def _(t, dc=dc, pp=pp, ntiles=ntiles, pT_t=pT_t):
                        k.w(t, "t", "dv", pT_t)
                        for j in range(ntiles):
                            ins = t.matmul(
                                pp[:, 0:128],
                                V_tm[:, j * 2048 + dc * 128:
                                     j * 2048 + (dc + 1) * 128],
                                pT[:, j * 128:(j + 1) * 128],
                                start=(j == 0), stop=(j == ntiles - 1))
                        k.i("pe", ins)
                    blk.tensor(_)
                    av_t = k.cnt["pe"]

                    def _(v, dc=dc, pp=pp, av_t=av_t):
                        k.w(v, "v", "pe", av_t)
                        if dc == 0:
                            o_buf.wait_free(k, v, "v")
                        k.i("dv", v.tensor_copy(osb[:, dc * 128:(dc + 1) * 128],
                                                pp[:, 0:128]))
                    blk.vector(_)

                    def _(t, tgt=k.cnt["dv"]):
                        k.w(t, "t", "dv", tgt)
                    blk.tensor(_)
                o_t = k.cnt["dv"]

                def _(sy, b=b, si=si, o_t=o_t):
                    k.w(sy, "s", "dv", o_t)
                    k.i("oS", sy.dma_start(
                        out=dram["Ob"][b].rearrange(
                            "p (dc f) -> p dc f", f=512)[:, :, si * 128:(si + 1) * 128],
                        in_=osb[:, :].rearrange("p (dc f) -> p dc f", f=128)), 16)
                blk.sync(_)
                o_buf.mark("oS", k.cnt["oS"])
            qb_buf.mark("pe", k.cnt["pe"])
        drain_dmas(k, blk)



def build_program(n_sub=4, dbg=()):
    nc = bass.Bass("TRN2", target_bir_lowering=False, debug=False)
    dram = {}

    def dt(name, shape, dtype, kind):
        return nc.dram_tensor(name, shape, dtype, kind=kind).ap()

    dram["hs"] = dt("hs", [T, D], F32, "ExternalInput")
    for nm in ("wqt", "wkt", "wvt", "wot", "opj"):
        dram[nm] = dt(nm, [D, D], F32, "ExternalInput")
    dram["qkv"] = dt("qkv", [D, 3 * D], F32, "ExternalInput")
    for nm in ("wg", "wu", "wd"):
        dram[nm] = dt(nm, [NE, D, D], F32, "ExternalInput")
    for nm in ("ga", "ua", "da"):
        dram[nm] = dt(nm, [2, D, RLO], F32, "ExternalInput")
    for nm in ("gb", "ub", "db"):
        dram[nm] = dt(nm, [2, RLO, D], F32, "ExternalInput")
    dram["wrp"] = dt("wrp", [128, 128], F32, "ExternalInput")
    dram["temp"] = dt("temp", [1, 1], F32, "ExternalInput")
    dram["gam_a_in"] = dt("gam_a_in", [128, 1], F32, "ExternalInput")
    dram["bet_a_in"] = dt("bet_a_in", [128, 1], F32, "ExternalInput")
    dram["gbg_in"] = dt("gbg_in", [128, 16], F32, "ExternalInput")
    dram["gbb_in"] = dt("gbb_in", [128, 16], F32, "ExternalInput")
    dram["identc"] = dt("identc", [128, 128], F32, "ExternalInput")
    dram["onesc"] = dt("onesc", [128, 128], F32, "ExternalInput")
    dram["epsc"] = dt("epsc", [1, 512], F32, "ExternalInput")
    amc = dt("amc", [4, 128, 512], F32, "ExternalInput")
    bmc = dt("bmc", [4, 128, 512], F32, "ExternalInput")
    dram["selc"] = dt("selc", [8, 1024], F32, "ExternalInput")
    dram["amc"] = [amc[m] for m in range(4)]
    dram["bmc"] = [bmc[m] for m in range(4)]
    dram["out"] = dt("out", [T, D], F32, "ExternalOutput")

    def internal(name, shape, dtype):
        kind = "ExternalOutput" if name in dbg else "Internal"
        return dt(name, shape, dtype, kind)

    dram["XA"] = internal("XA", [T, D], F32)
    dram["XB"] = internal("XB", [T, D], F32)
    dram["Qa"] = internal("Qa", [D, T], BF16)
    dram["Qb"] = internal("Qb", [D, T], BF16)
    dram["Hb"] = internal("Hb", [T, D], F32)
    oa = internal("Oa", [4, 128, 8192], BF16)
    ob = internal("Ob", [4, 128, 8192], BF16)
    dram["Oa"] = [oa[b] for b in range(4)]
    dram["Ob"] = [ob[b] for b in range(4)]
    for nm in ("cwg", "cwu", "cwd"):
        t_ = internal(nm, [NE, 4, 128, 8192], BF16)
        dram[nm] = [[t_[e][i] for i in range(4)] for e in range(NE)]

    cache_done = {}

    def tgt_name(i):
        if i == n_sub:
            return "out"
        return {1: "XA", 2: "XB", 3: "XA"}[i]

    with ExitStack() as ctx:
        E = ctx.enter_context
        k = KCtx(nc, ctx, SEMS)
        g = {}
        g["ident_bf"] = E(nc.sbuf_tensor("g_id", [128, 128], BF16))
        g["ones_bf"] = E(nc.sbuf_tensor("g_on", [128, 128], BF16))
        g["eps_bf"] = E(nc.sbuf_tensor("g_eps", [1, 512], BF16))
        g["gam_a"] = E(nc.sbuf_tensor("g_ga", [128, 1], F32))
        g["bet_a"] = E(nc.sbuf_tensor("g_ba", [128, 1], F32))
        g["gbg"] = E(nc.sbuf_tensor("g_gbg", [128, 16], F32))
        g["gbb"] = E(nc.sbuf_tensor("g_gbb", [128, 16], F32))

        with nc.Block("setup") as blk:
            def _(gp):
                k.i("cst", gp.dma_start(out=g["ident_bf"][:, :], in_=dram["identc"]), 16)
                k.i("cst", gp.dma_start(out=g["ones_bf"][:, :], in_=dram["onesc"]), 16)
                k.i("cst", gp.dma_start(out=g["eps_bf"][:, :], in_=dram["epsc"]), 16)
                k.i("cst", gp.dma_start(out=g["gam_a"][:, :], in_=dram["gam_a_in"]), 16)
                k.i("cst", gp.dma_start(out=g["bet_a"][:, :], in_=dram["bet_a_in"]), 16)
                k.i("cst", gp.dma_start(out=g["gbg"][:, :], in_=dram["gbg_in"]), 16)
                k.i("cst", gp.dma_start(out=g["gbb"][:, :], in_=dram["gbb_in"]), 16)
            blk.gpsimd(_)

            def _(sy):
                k.w(sy, "s", "cst")
            blk.sync(_)
        k.barrier()

        with ExitStack() as sA:
            g["K_fm"] = sA.enter_context(nc.sbuf_tensor("a_K", [128, 32768], BF16))
            g["V_tm"] = sA.enter_context(nc.sbuf_tensor("a_V", [128, 32768], BF16))
            phase_A1(nc, k, g, dram)
            k.barrier()
            phase_A2(nc, k, g, dram)
            k.barrier()
        phase_outproj(nc, k, g, dram, "A3", "Oa", "wot", "hs",
                      dram[tgt_name(1)])
        k.barrier()
        if n_sub >= 2:
            phase_FFN(nc, k, g, dram, cache_done, "F1", tgt_name(1),
                      dram[tgt_name(2)], 0)
            k.barrier()
        if n_sub >= 3:
            with ExitStack() as sB:
                g["K_fm"] = sB.enter_context(nc.sbuf_tensor("b_K", [128, 32768], BF16))
                g["V_tm"] = sB.enter_context(nc.sbuf_tensor("b_V", [128, 32768], BF16))
                phase_B1(nc, k, g, dram)
                k.barrier()
                phase_B2(nc, k, g, dram)
                k.barrier()
            phase_outproj(nc, k, g, dram, "B3", "Ob", "opj", "XB",
                          dram[tgt_name(3)], with_h="Hb")
            k.barrier()
        if n_sub >= 4:
            phase_FFN(nc, k, g, dram, cache_done, "F2", tgt_name(3),
                      dram[tgt_name(4)], 1)
            k.barrier()

        with nc.Block("fin") as blk:
            def _(sy):
                for nm in DMA_SEMS:
                    k.w(sy, "s", nm)
            blk.sync(_)

    return nc



def _host_consts():
    ident = np.eye(128, dtype=np.float32)
    ones = np.ones((128, 128), dtype=np.float32)
    epsr = np.full((1, 512), EPS, dtype=np.float32)
    r = np.arange(128)[:, None]
    c = np.arange(512)[None, :]
    am = np.stack([(r <= c - 128 * m).astype(np.float32) for m in range(4)])
    bm = np.stack([np.where(c <= 128 * m + r, 0.0, NEG_BIG).astype(np.float32)
                   for m in range(4)])
    sel = np.zeros((8, 1024), dtype=np.float32)
    for e in range(8):
        sel[e, e * 128:(e + 1) * 128] = 1.0
    return ident, ones, epsr, am, bm, sel


def make_common(inputs):
    ident, ones, epsr, am, bm, sel = _host_consts()
    wr = np.asarray(inputs["L0_ffn_Wr"], dtype=np.float32)
    wrp = np.ascontiguousarray(
        wr.reshape(16, 128, 8).transpose(1, 0, 2).reshape(128, 128))
    return {
        "wqt": np.ascontiguousarray(np.asarray(inputs["L0_attn_a_Wq"], dtype=np.float32).T),
        "wkt": np.ascontiguousarray(np.asarray(inputs["L0_attn_a_Wk"], dtype=np.float32).T),
        "wvt": np.ascontiguousarray(np.asarray(inputs["L0_attn_a_Wv"], dtype=np.float32).T),
        "wot": np.ascontiguousarray(np.asarray(inputs["L0_attn_a_Wo"], dtype=np.float32).T),
        "qkv": np.ascontiguousarray(np.asarray(inputs["L0_attn_b_qkv"], dtype=np.float32)),
        "opj": np.ascontiguousarray(np.asarray(inputs["L0_attn_b_o_proj"], dtype=np.float32)),
        "wg": np.ascontiguousarray(np.asarray(inputs["L0_ffn_Wg"], dtype=np.float32)),
        "wu": np.ascontiguousarray(np.asarray(inputs["L0_ffn_Wu"], dtype=np.float32)),
        "wd": np.ascontiguousarray(np.asarray(inputs["L0_ffn_Wd"], dtype=np.float32)),
        "ga": np.ascontiguousarray(np.asarray(inputs["L0_ffn_Ga"], dtype=np.float32)),
        "ua": np.ascontiguousarray(np.asarray(inputs["L0_ffn_Ua"], dtype=np.float32)),
        "da": np.ascontiguousarray(np.asarray(inputs["L0_ffn_Da"], dtype=np.float32)),
        "gb": np.ascontiguousarray(np.asarray(inputs["L0_ffn_Gb"], dtype=np.float32)),
        "ub": np.ascontiguousarray(np.asarray(inputs["L0_ffn_Ub"], dtype=np.float32)),
        "db": np.ascontiguousarray(np.asarray(inputs["L0_ffn_Db"], dtype=np.float32)),
        "wrp": wrp,
        "temp": np.asarray(inputs["L0_ffn_temperature"], dtype=np.float32).reshape(1, 1),
        "gam_a_in": np.asarray(inputs["L0_attn_a_gamma"], dtype=np.float32).reshape(128, 1).copy(),
        "bet_a_in": np.asarray(inputs["L0_attn_a_beta"], dtype=np.float32).reshape(128, 1).copy(),
        "gbg_in": np.ascontiguousarray(
            np.asarray(inputs["L0_attn_b_gamma"], dtype=np.float32).reshape(16, 128).T),
        "gbb_in": np.ascontiguousarray(
            np.asarray(inputs["L0_attn_b_beta"], dtype=np.float32).reshape(16, 128).T),
        "identc": ident, "onesc": ones, "epsc": epsr,
        "amc": am, "bmc": bm, "selc": sel,
    }


def kernel(**inputs):
    common = make_common(inputs)
    hs = np.asarray(inputs["L0_attn_a_hidden_states"], dtype=np.float32)
    nc = build_program(n_sub=4)
    in_maps = []
    for c in range(8):
        m = dict(common)
        m["hs"] = np.ascontiguousarray(hs[c])
        in_maps.append(m)
    res = run_bass_kernel_spmd(nc, in_maps, core_ids=list(range(8)))
    return np.stack([np.asarray(r["out"]) for r in res.results], axis=0)


# version 04
# speedup vs baseline: 1.0748x; 1.0748x over previous
import numpy as np
from contextlib import ExitStack

import concourse.bass as bass
import concourse.mybir as mybir
from concourse.bass_utils import run_bass_kernel_spmd

F32 = mybir.dt.float32
BF16 = mybir.dt.bfloat16
AF = mybir.ActivationFunctionType
ALU = mybir.AluOpType
AX = mybir.AxisListType

D = 2048
T = 2048
NDC = 16
H = 16
NE = 8
RLO = 64
EPS = 1e-5
GOLD_C = 0.36787944117144233
GOLD_L = 0.2123179275482191
GOLD_U = 0.5
QSCALE_A = float(128) ** -0.5
CSCALE_B = 1.0 / float(np.sqrt(np.float32(2048.0)))
NEG_BIG = -1e30
DBG = {}

SEMS = ["pe", "dv", "ac", "cst", "xL", "wA", "wB", "wC", "wD",
        "wAp", "wBp", "wCp", "wDp",
        "qL0", "qL1", "oL", "stX", "qS", "qS1", "hS", "oS",
        "cwA", "cwB", "cwC", "cwD"]
DMA_SEMS = SEMS[3:]
WSEM = {0: "wA", 1: "wB", 2: "wC", 3: "wD"}
WSEMP = {0: "wAp", 1: "wBp", 2: "wCp", 3: "wDp"}
CWSEM = {0: "cwA", 1: "cwB", 2: "cwC", 3: "cwD"}


class KCtx:
    ENGS = ("t", "v", "a", "s", "p")

    def __init__(self, nc, ctx, names):
        self.nc = nc
        self.sem = {}
        self.cnt = {}
        for nm in names:
            self.sem[nm] = ctx.enter_context(nc.semaphore("S_" + nm))
            self.cnt[nm] = 0
        self.waited = {}

    def i(self, nm, instr, n=1):
        instr.then_inc(self.sem[nm], n)
        self.cnt[nm] += n
        return self.cnt[nm]

    def w(self, eng, ek, nm, tgt=None):
        if tgt is None:
            tgt = self.cnt[nm]
        if tgt <= 0:
            return
        key = (ek, nm)
        if self.waited.get(key, -1) >= tgt:
            return
        eng.wait_ge(self.sem[nm], tgt)
        self.waited[key] = tgt

    def barrier(self):
        for ek in self.ENGS:
            for nm in self.sem:
                self.waited[(ek, nm)] = self.cnt[nm]


class Buf:
    def __init__(self):
        self.rel = []

    def wait_free(self, k, eng, ek):
        for nm, tgt in self.rel:
            k.w(eng, ek, nm, tgt)
        self.rel = []

    def mark(self, nm, tgt):
        self.rel.append((nm, tgt))



def emit_ln_tile(k, blk, xt_ap, out_ap, scr):
    stats, mv, vv, rstd = scr["stats"], scr["mv"], scr["vv"], scr["rstd"]

    def _(v):
        k.w(v, "v", "ac")
        k.w(v, "v", "dv")
        ins = None
        for i in range(4):
            ins = v.bn_stats(stats[:, 6 * i:6 * i + 6],
                             xt_ap[:, 512 * i:512 * (i + 1)])
        k.i("dv", ins)
        k.w(v, "v", "dv")
        k.i("dv", v.bn_aggr(mv[:, :], stats[:, :]))
        k.w(v, "v", "dv")
        k.i("dv", v.tensor_scalar(vv[:, :], mv[:, 1:2], EPS, None, ALU.add))
    blk.vector(_)

    def _(sc):
        k.w(sc, "a", "dv")
        k.i("ac", sc.activation(rstd[:, :], vv[:, :], AF.Sqrt))
    blk.scalar(_)

    def _(v):
        k.w(v, "v", "ac")
        k.i("dv", v.reciprocal(rstd[:, :], rstd[:, :]))
        k.w(v, "v", "dv")
        k.i("dv", v.tensor_scalar(out_ap, xt_ap, mv[:, 0:1], rstd[:, 0:1],
                                  ALU.subtract, ALU.mult))
    blk.vector(_)
    return k.cnt["dv"]


def emit_transpose_tile(k, blk, src_bf16, hfm_write, tp_ps, tp_state, ident_bf,
                        affine=None):
    for dc in range(NDC):
        bank = dc % 2
        ps = tp_ps[bank]

        def _(t, bank=bank, ps=ps, dc=dc):
            if bank in tp_state:
                nm, v = tp_state[bank]
                k.w(t, "t", nm, v)
            k.i("pe", t.transpose(ps[:, 0:128],
                                  src_bf16[:, dc * 128:(dc + 1) * 128],
                                  ident_bf[:, :]))
        blk.tensor(_)

        if affine is None:
            def _(v, dc=dc, bank=bank, ps=ps):
                k.w(v, "v", "pe")
                tgt = k.i("dv", v.tensor_copy(hfm_write(dc), ps[:, 0:128]))
                tp_state[bank] = ("dv", tgt)
            blk.vector(_)
        else:
            gs_fn, bb_fn = affine

            def _(v, dc=dc, bank=bank, ps=ps):
                k.w(v, "v", "pe")
                tgt = k.i("dv", v.tensor_scalar(
                    hfm_write(dc), ps[:, 0:128], gs_fn(dc), bb_fn(dc),
                    ALU.mult, ALU.add))
                tp_state[bank] = ("dv", tgt)
            blk.vector(_)


def load_cb_raw(k, blk, src4d, cb, buf, wsem):
    def _(gp):
        buf.wait_free(k, gp, "p")
        for dcg in range(4):
            k.i(wsem, gp.dma_start(
                out=cb[:, dcg * 2048:(dcg + 1) * 2048].rearrange(
                    "p (dc f) -> p dc f", f=512),
                in_=src4d[:, dcg * 4:(dcg + 1) * 4, :]), 16)
    blk.gpsimd(_)
    return (wsem, k.cnt[wsem])


def drain_dmas(k, blk):
    def _(sy):
        for nm in DMA_SEMS:
            k.w(sy, "s", nm)
    blk.sync(_)



def phase_A1(nc, k, g, dram):
    with ExitStack() as ctx:
        E = ctx.enter_context
        xb = E(nc.sbuf_tensor("a1_xb", [128, 2048], F32))
        h_tm = E(nc.sbuf_tensor("a1_htm", [128, 2048], BF16))
        h_fm = E(nc.sbuf_tensor("a1_hfm", [128, 8192], BF16))
        cb0 = E(nc.sbuf_tensor("a1_cb0", [128, 8192], BF16))
        cb1 = E(nc.sbuf_tensor("a1_cb1", [128, 8192], BF16))
        esq = E(nc.sbuf_tensor("a1_esq", [128, 2048], BF16))
        murs = E(nc.sbuf_tensor("a1_murs", [1, 2048], BF16))
        rowf = E(nc.sbuf_tensor("a1_rowf", [1, 1536], F32))
        d1 = E(nc.sbuf_tensor("a1_d1", [128, 512], BF16))
        qo = E(nc.sbuf_tensor("a1_qo", [128, 1024], BF16))
        stats = E(nc.sbuf_tensor("a1_st", [128, 24], F32))
        mv = E(nc.sbuf_tensor("a1_mv", [128, 2], F32))
        vv = E(nc.sbuf_tensor("a1_vv", [128, 1], F32))
        rstd = E(nc.sbuf_tensor("a1_rs", [128, 1], F32))
        tp0 = E(nc.psum_tensor("a1_tp0", [128, 1024], BF16))
        tp1 = E(nc.psum_tensor("a1_tp1", [128, 1024], BF16))
        pq0 = E(nc.psum_tensor("a1_pq0", [128, 512], F32))
        pq1 = E(nc.psum_tensor("a1_pq1", [128, 512], F32))
        pv = E(nc.psum_tensor("a1_pv", [128, 512], F32))
        st1 = E(nc.psum_tensor("a1_s1", [1, 512], F32))
        st2 = E(nc.psum_tensor("a1_s2", [1, 512], F32))
        bps = E(nc.psum_tensor("a1_b", [128, 512], F32))
        blk = E(nc.Block("A1"))

        scr = {"stats": stats, "mv": mv, "vv": vv, "rstd": rstd}
        K_fm, V_tm = g["K_fm"], g["V_tm"]
        ident_bf = g["ident_bf"]
        ones_bf = g["ones_bf"]
        gam_a, bet_a = g["gam_a"], g["bet_a"]
        pqs = (pq0, pq1)
        cbbuf = [Buf(), Buf()]
        xb_buf = Buf()
        htm_buf = Buf()
        tp_state = {}
        wsrc = {w: dram[w].rearrange("(dc p) f -> p dc f", p=128)
                for w in ("wqt", "wkt", "wvt")}

        pipe = {}

        def S0(n, job):
            wname, cb, sem, tgt, head, scol, fi = job
            par = n % 2

            def _(t):
                k.w(t, "t", sem, tgt)
                k.w(t, "t", "dv", pipe["hfm_ready"])
                prev = pipe.get(("e", n - 2))
                if prev:
                    k.w(t, "t", "dv", prev)
                for dc in range(NDC):
                    ins = t.matmul(
                        pqs[par][:, :],
                        cb[:, dc * 512 + fi * 128:dc * 512 + (fi + 1) * 128],
                        h_fm[:, dc * 512:(dc + 1) * 512],
                        start=(dc == 0), stop=(dc == NDC - 1))
                k.i("pe", ins)
            blk.tensor(_)
            pipe[("mm", n)] = k.cnt["pe"]

        def S1(n, job):
            par = n % 2
            eo = par * 1024

            def _(v):
                k.w(v, "v", "pe", pipe[("mm", n)])
                prev = pipe.get(("apply", n - 2))
                if prev:
                    k.w(v, "v", "dv", prev)
                k.i("dv", v.tensor_scalar(esq[:, eo:eo + 512], pqs[par][:, :],
                                          gam_a[:, 0:1], bet_a[:, 0:1],
                                          ALU.mult, ALU.add))
                pipe[("e", n)] = k.cnt["dv"]
                k.w(v, "v", "dv")
                k.i("dv", v.tensor_tensor(esq[:, eo + 512:eo + 1024],
                                          esq[:, eo:eo + 512],
                                          esq[:, eo:eo + 512], ALU.mult))
            blk.vector(_)
            pipe[("e2", n)] = k.cnt["dv"]

        def S2(n, job):
            par = n % 2
            eo = par * 1024

            def _(t):
                k.w(t, "t", "dv", pipe[("e2", n)])
                prev = pipe.get(("stfree", n - 1))
                if prev:
                    k.w(t, "t", "dv", prev)
                t.matmul(st1[:, :], ones_bf[:, 0:1],
                         esq[:, eo:eo + 512], start=True, stop=True)
                k.i("pe", t.matmul(st2[:, :], ones_bf[:, 0:1],
                                   esq[:, eo + 512:eo + 1024],
                                   start=True, stop=True))
            blk.tensor(_)
            pipe[("st", n)] = k.cnt["pe"]

        def S3(n, job):
            par = n % 2
            ro = 0

            def _(v):
                k.w(v, "v", "pe", pipe[("st", n)])
                k.w(v, "v", "dv")
                k.w(v, "v", "ac")
                k.i("dv", v.tensor_scalar(rowf[0:1, ro:ro + 512], st1[0:1, :],
                                          1.0 / 128, None, ALU.mult))
                k.w(v, "v", "dv")
                k.i("dv", v.tensor_tensor(rowf[0:1, ro + 1024:ro + 1536],
                                          rowf[0:1, ro:ro + 512],
                                          rowf[0:1, ro:ro + 512], ALU.mult))
                k.w(v, "v", "dv")
                k.i("dv", v.scalar_tensor_tensor(rowf[0:1, ro + 512:ro + 1024],
                                                 st2[0:1, :], 1.0 / 128,
                                                 rowf[0:1, ro + 1024:ro + 1536],
                                                 ALU.mult, ALU.subtract))
                pipe[("stfree", n)] = k.cnt["dv"]
                k.w(v, "v", "dv")
                k.i("dv", v.tensor_scalar(rowf[0:1, ro + 512:ro + 1024],
                                          rowf[0:1, ro + 512:ro + 1024],
                                          EPS, None, ALU.add))
            blk.vector(_)

            def _(sc):
                k.w(sc, "a", "dv")
                k.i("ac", sc.activation(rowf[0:1, ro + 512:ro + 1024],
                                        rowf[0:1, ro + 512:ro + 1024], AF.Sqrt))
            blk.scalar(_)

            mo = par * 1024

            def _(v):
                k.w(v, "v", "ac")
                k.i("dv", v.reciprocal(rowf[0:1, ro + 512:ro + 1024],
                                       rowf[0:1, ro + 512:ro + 1024]))
                k.w(v, "v", "dv")
                k.i("dv", v.tensor_copy(murs[0:1, mo:mo + 256],
                                        rowf[0:1, ro:ro + 256]))
                k.i("dv", v.tensor_copy(murs[0:1, mo + 256:mo + 512],
                                        rowf[0:1, ro + 512:ro + 768]))
                k.i("dv", v.tensor_copy(murs[0:1, mo + 512:mo + 768],
                                        rowf[0:1, ro + 256:ro + 512]))
                k.i("dv", v.tensor_copy(murs[0:1, mo + 768:mo + 1024],
                                        rowf[0:1, ro + 768:ro + 1024]))
            blk.vector(_)
            pipe[("murs", n)] = k.cnt["dv"]

        def S4(n, job):
            wname, cb, sem, tgt, head, scol, fi = job
            par = n % 2
            eo = par * 1024
            mo = par * 1024
            do = par * 256
            qoo = par * 512
            out_is_k = (wname == "wkt")
            for half in range(2):
                def _(t, half=half):
                    k.w(t, "t", "dv", pipe[("murs", n)])
                    prev = pipe.get(("apply_h", n, half - 1)) or pipe.get(
                        ("apply", n - 1))
                    if prev:
                        k.w(t, "t", "dv", prev)
                    k.i("pe", t.matmul(bps[:, :], ones_bf[0:1, 0:128],
                                       murs[0:1, mo + half * 512:mo + (half + 1) * 512],
                                       start=True, stop=True))
                blk.tensor(_)
                b_t = k.cnt["pe"]

                def _(v, half=half, b_t=b_t):
                    k.w(v, "v", "pe", b_t)
                    k.i("dv", v.tensor_tensor(
                        d1[:, do:do + 256],
                        esq[:, eo + half * 256:eo + (half + 1) * 256],
                        bps[:, 0:256], ALU.subtract))
                    k.w(v, "v", "dv")
                    if out_is_k:
                        dst = K_fm[:, head * 2048 + scol + half * 256:
                                   head * 2048 + scol + (half + 1) * 256]
                    else:
                        dst = qo[:, qoo + half * 256:qoo + (half + 1) * 256]
                        if half == 0:
                            prevq = pipe.get(("qspill", n - 2))
                            if prevq:
                                k.w(v, "v", "qS" if n % 2 == 0 else "qS1",
                                    prevq)
                    k.i("dv", v.tensor_tensor(dst, d1[:, do:do + 256],
                                              bps[:, 256:512], ALU.mult))
                blk.vector(_)
                pipe[("apply_h", n, half)] = k.cnt["dv"]
            pipe[("apply", n)] = k.cnt["dv"]

            if not out_is_k:
                qsn = "qS" if n % 2 == 0 else "qS1"

                def _(sy, qsn=qsn):
                    k.w(sy, "s", "dv", pipe[("apply", n)])
                    k.i(qsn, sy.dma_start(
                        out=dram["Qa"][head * 128:(head + 1) * 128,
                                       scol:scol + 512],
                        in_=qo[:, qoo:qoo + 512]), 16)
                blk.sync(_)
                pipe[("qspill", n)] = k.cnt[qsn]

        for s in range(4):
            scol = s * 512
            for ti in range(4):
                row0 = scol + ti * 128

                def _(sy, row0=row0):
                    xb_buf.wait_free(k, sy, "s")
                    k.i("xL", sy.dma_start(out=xb[:, :],
                                           in_=dram["hs"][row0:row0 + 128, :]), 16)
                blk.sync(_)
                xld = k.cnt["xL"]

                def _(v, xld=xld):
                    k.w(v, "v", "xL", xld)
                    htm_buf.wait_free(k, v, "v")
                blk.vector(_)
                t = emit_ln_tile(k, blk, xb[:, :], h_tm[:, :], scr)
                xb_buf.mark("dv", t)

                def hw(dc, ti=ti):
                    return h_fm[:, dc * 512 + ti * 128:dc * 512 + (ti + 1) * 128]

                def _(t_, tgt=t):
                    k.w(t_, "t", "dv", tgt)
                blk.tensor(_)
                emit_transpose_tile(k, blk, h_tm, hw, [tp0, tp1], tp_state,
                                    ident_bf)
                htm_buf.mark("pe", k.cnt["pe"])
            pipe.clear()
            pipe["hfm_ready"] = k.cnt["dv"]
            if k.cnt["qS"]:
                pipe[("qspill", -2)] = k.cnt["qS"]
            if k.cnt["qS1"]:
                pipe[("qspill", -1)] = k.cnt["qS1"]

            jobs = []
            for wi, wname in enumerate(("wqt", "wkt")):
                for cbg in range(4):
                    for fi in range(4):
                        jobs.append((wname, cbg, fi))
            N = len(jobs)
            jinfo = {}

            def emit_slot(n):
                if n < N:
                    wname, cbg, fi = jobs[n]
                    if fi == 0:
                        wi = 0 if wname == "wqt" else 1
                        cbp = (wi * 4 + cbg) % 2
                        cb = (cb0, cb1)[cbp]
                        sem, tgt = load_cb_raw(
                            k, blk,
                            wsrc[wname][:, :, cbg * 512:(cbg + 1) * 512],
                            cb, cbbuf[cbp], WSEMP[cbp])
                        jinfo[(wname, cbg)] = (cb, sem, tgt, cbp)
                    cb, sem, tgt, cbp = jinfo[(wname, cbg)]
                    job = (wname, cb, sem, tgt, cbg * 4 + fi, scol, fi)
                    jinfo[n] = job
                    S0(n, job)
                    if fi == 3:
                        cbbuf[cbp].mark("pe", k.cnt["pe"])
                if 0 <= n - 1 < N:
                    S2(n - 1, jinfo[n - 1])
                if 0 <= n - 2 < N:
                    S4(n - 2, jinfo[n - 2])
                if n < N:
                    S1(n, jinfo[n])
                if 0 <= n - 1 < N:
                    S3(n - 1, jinfo[n - 1])

            for n in range(N + 2):
                emit_slot(n)

            for cbg in range(4):
                cbp = cbg % 2
                cb = (cb0, cb1)[cbp]
                sem, tgt = load_cb_raw(
                    k, blk, wsrc["wvt"][:, :, cbg * 512:(cbg + 1) * 512],
                    cb, cbbuf[cbp], WSEMP[cbp])
                for tc in range(4):
                    tcg = s * 4 + tc

                    def _(t, tc=tc, cb=cb, sem=sem, tgt=tgt):
                        k.w(t, "t", sem, tgt)
                        k.w(t, "t", "dv", pipe["hfm_ready"])
                        prev = pipe.get("v_evac")
                        if prev:
                            k.w(t, "t", "dv", prev)
                        for dc in range(NDC):
                            ins = t.matmul(
                                pv[:, :],
                                h_fm[:, dc * 512 + tc * 128:
                                     dc * 512 + (tc + 1) * 128],
                                cb[:, dc * 512:(dc + 1) * 512],
                                start=(dc == 0), stop=(dc == NDC - 1))
                        k.i("pe", ins)
                    blk.tensor(_)
                    mm_t = k.cnt["pe"]

                    def _(v, tcg=tcg, cbg=cbg, mm_t=mm_t):
                        k.w(v, "v", "pe", mm_t)
                        k.i("dv", v.tensor_copy(
                            V_tm[:, tcg * 2048 + cbg * 512:
                                 tcg * 2048 + (cbg + 1) * 512],
                            pv[:, :]))
                    blk.vector(_)
                    pipe["v_evac"] = k.cnt["dv"]
                cbbuf[cbp].mark("pe", k.cnt["pe"])
        drain_dmas(k, blk)



def phase_A2(nc, k, g, dram):
    with ExitStack() as ctx:
        E = ctx.enter_context
        qh = E(nc.sbuf_tensor("a2_qh", [128, 1024], BF16))
        at = E(nc.sbuf_tensor("a2_at", [128, 1024], BF16))
        at2 = E(nc.sbuf_tensor("a2_at2", [128, 1024], BF16))
        zi = E(nc.sbuf_tensor("a2_zi", [128, 512], F32))
        ofm = E(nc.sbuf_tensor("a2_ofm", [128, 8192], BF16))
        am = E(nc.sbuf_tensor("a2_am", [128, 2048], BF16))
        psc0 = E(nc.psum_tensor("a2_s0", [128, 512], F32))
        psc1 = E(nc.psum_tensor("a2_s1", [128, 512], F32))
        psz0 = E(nc.psum_tensor("a2_z0", [128, 512], F32))
        psz1 = E(nc.psum_tensor("a2_z1", [128, 512], F32))
        pso0 = E(nc.psum_tensor("a2_oo0", [128, 512], F32))
        pso1 = E(nc.psum_tensor("a2_oo1", [128, 512], F32))
        blk = E(nc.Block("A2"))

        K_fm, V_tm = g["K_fm"], g["V_tm"]
        ones_bf, eps_bf = g["ones_bf"], g["eps_bf"]
        psc = (psc0, psc1)
        pszs = (psz0, psz1)
        psos = (pso0, pso1)
        oz_evac = {}
        qh_buf = [Buf(), Buf()]
        ofm_buf = Buf()

        def _(gp):
            for m in range(4):
                k.i("cst", gp.dma_start(out=am[:, m * 512:(m + 1) * 512],
                                        in_=dram["amc"][m]), 16)
        blk.gpsimd(_)
        am_t = k.cnt["cst"]

        for b in range(4):
            J = 4 * b + 4
            for h in range(H):
                qp = h % 2
                hp = (b * H + h) % 2
                psz = pszs[hp]
                pso = psos[hp]
                qsem = "qL0" if qp == 0 else "qL1"

                def _(sa, h=h, b=b, qp=qp, qsem=qsem):
                    qh_buf[qp].wait_free(k, sa, "a")
                    k.i(qsem, sa.dma_start(
                        out=qh[:, qp * 512:(qp + 1) * 512],
                        in_=dram["Qa"][h * 128:(h + 1) * 128,
                                       b * 512:(b + 1) * 512]), 16)
                blk.scalar(_)
                q_t = k.cnt[qsem]

                sc_t = {}
                sq_done = [0] * J
                sq_sem = ["ac"] * J
                oz_pe = [0] * J

                def emit_sc(j):
                    def _(t, j=j):
                        k.w(t, "t", qsem, q_t)
                        k.w(t, "t", "cst", am_t)
                        k.i("pe", t.matmul(
                            psc[j % 2][:, :],
                            K_fm[:, h * 2048 + j * 128:h * 2048 + (j + 1) * 128],
                            qh[:, qp * 512:(qp + 1) * 512],
                            start=True, stop=True))
                        sc_t[j] = k.cnt["pe"]
                    blk.tensor(_)

                def emit_sq(j):
                    def _(v, j=j):
                        if j >= 2:
                            k.w(v, "v", "pe", oz_pe[j - 2])
                        k.w(v, "v", "pe", sc_t[j])
                        sl = slice((j % 2) * 512, (j % 2 + 1) * 512)
                        k.i("dv", v.tensor_scalar(at2[:, sl], psc[j % 2][:, :],
                                                  QSCALE_A, None, ALU.mult))
                        k.w(v, "v", "dv")
                        k.i("dv", v.tensor_tensor(at[:, sl], at2[:, sl],
                                                  at2[:, sl], ALU.mult))
                        if j >= 4 * b:
                            mm = j - 4 * b
                            k.w(v, "v", "dv")
                            k.i("dv", v.tensor_tensor(
                                at[:, sl], at[:, sl],
                                am[:, mm * 512:(mm + 1) * 512], ALU.mult))
                        sq_done[j] = k.cnt["dv"]
                        sq_sem[j] = "dv"
                    blk.vector(_)

                def emit_oz(j):
                    def _(t, j=j):
                        k.w(t, "t", sq_sem[j], sq_done[j])
                        if j == 0:
                            prev = oz_evac.get(hp)
                            if prev:
                                k.w(t, "t", "dv", prev)
                        ins = t.matmul(
                            pso[:, :],
                            V_tm[:, j * 2048 + h * 128:j * 2048 + (h + 1) * 128],
                            at[:, (j % 2) * 512:(j % 2 + 1) * 512],
                            start=(j == 0), stop=(j == J - 1))
                        ins = t.matmul(
                            psz[:, :], ones_bf[:, 0:128],
                            at[:, (j % 2) * 512:(j % 2 + 1) * 512],
                            start=(j == 0), stop=False)
                        if j == J - 1:
                            ins = t.matmul(psz[:, :], ones_bf[0:1, 0:128],
                                           eps_bf[0:1, :],
                                           start=False, stop=True)
                        k.i("pe", ins)
                        oz_pe[j] = k.cnt["pe"]
                    blk.tensor(_)

                emit_sc(0)
                emit_sq(0)
                if J > 1:
                    emit_sc(1)
                    emit_sq(1)
                for j in range(J):
                    emit_oz(j)
                    if j + 2 < J:
                        emit_sc(j + 2)
                        emit_sq(j + 2)
                oz_t = k.cnt["pe"]
                qh_buf[qp].mark("pe", oz_t)

                def _(v, h=h, oz_t=oz_t):
                    k.w(v, "v", "pe", oz_t)
                    k.i("dv", v.reciprocal(zi[:, :], psz[:, :]))
                    k.w(v, "v", "dv")
                    if h == 0:
                        ofm_buf.wait_free(k, v, "v")
                    k.i("dv", v.tensor_tensor(ofm[:, h * 512:(h + 1) * 512],
                                              pso[:, :], zi[:, :], ALU.mult))
                blk.vector(_)
                oz_evac[hp] = k.cnt["dv"]

            def _(sy, b=b):
                k.w(sy, "s", "dv")
                k.i("oS", sy.dma_start(out=dram["Oa"][b], in_=ofm[:, :]), 16)
            blk.sync(_)
            ofm_buf.mark("oS", k.cnt["oS"])
        drain_dmas(k, blk)



def phase_outproj(nc, k, g, dram, name, osp_name, wname, x_in_name, x_out,
                  with_h=None):
    with ExitStack() as ctx:
        E = ctx.enter_context
        ofb = E(nc.sbuf_tensor(name + "_ofb", [128, 8192], BF16))
        cb0 = E(nc.sbuf_tensor(name + "_cb0", [128, 8192], BF16))
        cb1 = E(nc.sbuf_tensor(name + "_cb1", [128, 8192], BF16))
        xt = E(nc.sbuf_tensor(name + "_xt", [128, 8192], F32))
        ht = E(nc.sbuf_tensor(name + "_ht", [128, 8192], F32)) if with_h else None
        x1 = E(nc.sbuf_tensor(name + "_x1", [128, 8192], F32))
        po0 = E(nc.psum_tensor(name + "_p0", [128, 512], F32))
        po1 = E(nc.psum_tensor(name + "_p1", [128, 512], F32))
        blk = E(nc.Block(name))

        wsrc = dram[wname].rearrange("(dc p) f -> p dc f", p=128)
        cbbuf = [Buf(), Buf()]
        ofb_buf = Buf()
        xt_buf = Buf()
        x1_buf = Buf()
        po = (po0, po1)
        po_evac = {}
        x_in = dram[x_in_name]

        for b in range(4):
            def _(sa, b=b):
                ofb_buf.wait_free(k, sa, "a")
                k.i("oL", sa.dma_start(out=ofb[:, :], in_=dram[osp_name][b]), 16)
            blk.scalar(_)
            o_t = k.cnt["oL"]

            def _(sy, b=b):
                xt_buf.wait_free(k, sy, "s")
                for sc_i in range(4):
                    row = b * 512 + sc_i * 128
                    k.i("xL", sy.dma_start(
                        out=xt[:, sc_i * 2048:(sc_i + 1) * 2048],
                        in_=x_in[row:row + 128, :]), 16)
                if with_h:
                    for sc_i in range(4):
                        row = b * 512 + sc_i * 128
                        k.i("xL", sy.dma_start(
                            out=ht[:, sc_i * 2048:(sc_i + 1) * 2048],
                            in_=dram[with_h][row:row + 128, :]), 16)
            blk.sync(_)
            x_t = k.cnt["xL"]

            for j in range(4):
                cbp = j % 2
                cb = (cb0, cb1)[cbp]
                sem, tgt = load_cb_raw(k, blk,
                                       wsrc[:, :, j * 512:(j + 1) * 512],
                                       cb, cbbuf[cbp], WSEMP[cbp])
                for sc_i in range(4):
                    pp = po[(j * 4 + sc_i) % 2]

                    def _(t, j=j, sc_i=sc_i, pp=pp, cb=cb, sem=sem, tgt=tgt,
                          o_t=o_t):
                        k.w(t, "t", sem, tgt)
                        k.w(t, "t", "oL", o_t)
                        pv_ = po_evac.get((j * 4 + sc_i) % 2)
                        if pv_:
                            k.w(t, "t", "dv", pv_)
                        for dc in range(NDC):
                            ins = t.matmul(
                                pp[:, :],
                                ofb[:, dc * 512 + sc_i * 128:
                                    dc * 512 + (sc_i + 1) * 128],
                                cb[:, dc * 512:(dc + 1) * 512],
                                start=(dc == 0), stop=(dc == NDC - 1))
                        k.i("pe", ins)
                    blk.tensor(_)
                    mm_t = k.cnt["pe"]

                    def _(v, j=j, sc_i=sc_i, pp=pp, mm_t=mm_t, x_t=x_t):
                        k.w(v, "v", "pe", mm_t)
                        k.w(v, "v", "xL", x_t)
                        if j == 0 and sc_i == 0:
                            x1_buf.wait_free(k, v, "v")
                        dst = x1[:, sc_i * 2048 + j * 512:sc_i * 2048 + (j + 1) * 512]
                        if with_h:
                            k.i("dv", v.tensor_tensor(
                                dst, pp[:, :],
                                ht[:, sc_i * 2048 + j * 512:
                                   sc_i * 2048 + (j + 1) * 512], ALU.add))
                            k.w(v, "v", "dv")
                            k.i("dv", v.tensor_tensor(
                                dst, dst,
                                xt[:, sc_i * 2048 + j * 512:
                                   sc_i * 2048 + (j + 1) * 512], ALU.add))
                        else:
                            k.i("dv", v.tensor_tensor(
                                dst, pp[:, :],
                                xt[:, sc_i * 2048 + j * 512:
                                   sc_i * 2048 + (j + 1) * 512], ALU.add))
                    blk.vector(_)
                    po_evac[(j * 4 + sc_i) % 2] = k.cnt["dv"]
                cbbuf[cbp].mark("pe", k.cnt["pe"])
            ev_t = k.cnt["dv"]
            xt_buf.mark("dv", ev_t)
            ofb_buf.mark("pe", k.cnt["pe"])

            def _(sy, b=b, ev_t=ev_t):
                k.w(sy, "s", "dv", ev_t)
                for sc_i in range(4):
                    row = b * 512 + sc_i * 128
                    k.i("stX", sy.dma_start(
                        out=x_out[row:row + 128, :],
                        in_=x1[:, sc_i * 2048:(sc_i + 1) * 2048]), 16)
            blk.sync(_)
            x1_buf.mark("stX", k.cnt["stX"])
        drain_dmas(k, blk)



def phase_FFN(nc, k, g, dram, cache_done, name, x_in_name, x_out, call_idx):
    with ExitStack() as ctx:
        E = ctx.enter_context
        x_acc = E(nc.sbuf_tensor(name + "_xa", [128, 8192], F32))
        h_fm = E(nc.sbuf_tensor(name + "_hf", [128, 8192], BF16))
        mid = E(nc.sbuf_tensor(name + "_mid", [128, 8192], BF16))
        cbs = [E(nc.sbuf_tensor(name + f"_cb{i}", [128, 8192], BF16))
               for i in range(4)]
        h_tm = E(nc.sbuf_tensor(name + "_htm", [128, 2048], BF16))
        c1 = E(nc.sbuf_tensor(name + "_c1", [128, 2048], BF16))
        c2 = E(nc.sbuf_tensor(name + "_c2", [128, 2048], BF16))
        pw_sb = E(nc.sbuf_tensor(name + "_pws", [128, 512], F32))
        ga_sb = E(nc.sbuf_tensor(name + "_ga", [128, 2048], BF16))
        ua_sb = E(nc.sbuf_tensor(name + "_ua", [128, 2048], BF16))
        da_sb = E(nc.sbuf_tensor(name + "_da", [128, 2048], BF16))
        gb_sb = E(nc.sbuf_tensor(name + "_gb", [64, 4096], BF16))
        ub_sb = E(nc.sbuf_tensor(name + "_ub", [64, 4096], BF16))
        db_sb = E(nc.sbuf_tensor(name + "_db", [64, 4096], BF16))
        ya_g = E(nc.sbuf_tensor(name + "_yag", [64, 512], BF16))
        ya_u = E(nc.sbuf_tensor(name + "_yau", [64, 512], BF16))
        za = E(nc.sbuf_tensor(name + "_za", [64, 512], BF16))
        wr_bf = E(nc.sbuf_tensor(name + "_wr", [128, 128], BF16))
        sel_sb = E(nc.sbuf_tensor(name + "_sel", [8, 1024], F32))
        id32 = E(nc.sbuf_tensor(name + "_id", [128, 128], F32))
        on32 = E(nc.sbuf_tensor(name + "_on", [128, 128], F32))
        w_tm = E(nc.sbuf_tensor(name + "_wtm", [128, 32], F32))
        w_fm = E(nc.sbuf_tensor(name + "_wfm", [8, 512], F32))
        rsc = E(nc.sbuf_tensor(name + "_rsc", [128, 128], F32))
        rs1 = E(nc.sbuf_tensor(name + "_rs1", [128, 8], F32))
        tmp_sb = E(nc.sbuf_tensor(name + "_tmp", [1, 1], F32))
        tinv = E(nc.sbuf_tensor(name + "_ti", [128, 1], F32))
        stats = E(nc.sbuf_tensor(name + "_st", [128, 24], F32))
        mv = E(nc.sbuf_tensor(name + "_mv", [128, 2], F32))
        vv = E(nc.sbuf_tensor(name + "_vv", [128, 1], F32))
        rstd = E(nc.sbuf_tensor(name + "_rsd", [128, 1], F32))
        pw = E(nc.psum_tensor(name + "_pw", [128, 512], F32))
        pg0 = E(nc.psum_tensor(name + "_pg0", [128, 512], F32))
        pg1 = E(nc.psum_tensor(name + "_pg1", [128, 512], F32))
        pu0 = E(nc.psum_tensor(name + "_pu0", [128, 512], F32))
        pu1 = E(nc.psum_tensor(name + "_pu1", [128, 512], F32))
        pd0 = E(nc.psum_tensor(name + "_pd0", [128, 512], F32))
        pd1 = E(nc.psum_tensor(name + "_pd1", [128, 512], F32))
        blk = E(nc.Block(name))

        scr = {"stats": stats, "mv": mv, "vv": vv, "rstd": rstd}
        ident_bf = g["ident_bf"]
        x_in = dram[x_in_name]
        cbbuf = [Buf() for _ in range(4)]
        htm_buf = Buf()
        xacc_buf = Buf()
        pg = (pg0, pg1)
        pu = (pu0, pu1)
        pd = (pd0, pd1)
        tp_ps = [pd0[:, :].bitcast(BF16), pd1[:, :].bitcast(BF16)]
        wgsrc = dram["wg"].rearrange("e (dc p) f -> e p dc f", p=128)
        wusrc = dram["wu"].rearrange("e (dc p) f -> e p dc f", p=128)
        wdsrc = dram["wd"].rearrange("e (fc p) d -> e p fc d", p=128)

        def _(gp):
            k.i("cst", gp.dma_start(out=wr_bf[:, :], in_=dram["wrp"]), 16)
            k.i("cst", gp.dma_start(out=id32[:, :], in_=dram["identc"]), 16)
            k.i("cst", gp.dma_start(out=on32[:, :], in_=dram["onesc"]), 16)
            k.i("cst", gp.dma_start(out=sel_sb[:, :], in_=dram["selc"]), 16)
            for nm_, dst in (("ga", ga_sb), ("ua", ua_sb), ("da", da_sb)):
                src = dram[nm_].rearrange("l (dc p) r -> p l dc r", p=128)
                k.i("cst", gp.dma_start(
                    out=dst[:, :].rearrange("p (l dc r) -> p l dc r", l=2, r=RLO),
                    in_=src), 16)
            for nm_, dst in (("gb", gb_sb), ("ub", ub_sb), ("db", db_sb)):
                src = dram[nm_].rearrange("l r f -> r l f")
                k.i("cst", gp.dma_start(
                    out=dst[:, :].rearrange("r (l f) -> r l f", l=2), in_=src), 16)
            k.i("cst", gp.dma_start(out=tmp_sb[:, :], in_=dram["temp"]), 16)
        blk.gpsimd(_)
        smalls_t = k.cnt["cst"]

        def _(v):
            k.w(v, "v", "cst", smalls_t)
            k.i("dv", v.reciprocal(tmp_sb[:, :], tmp_sb[:, :]))
        blk.vector(_)

        def _(t):
            k.w(t, "t", "dv")
            k.i("pe", t.matmul(pw[:, 0:1], on32[0:1, 0:128], tmp_sb[0:1, 0:1],
                               start=True, stop=True))
        blk.tensor(_)

        def _(v):
            k.w(v, "v", "pe")
            k.i("dv", v.tensor_copy(tinv[:, :], pw[:, 0:1]))
        blk.vector(_)
        tinv_t = k.cnt["dv"]
        prev_expert_evac = 0

        def load_cb(kind, e, cbg, bi, queue):
            tag = (kind, e, cbg)
            cb = cbs[bi]
            buf = cbbuf[bi]
            wsem = WSEM[bi]
            if tag not in cache_done:
                src = {"wg": wgsrc, "wu": wusrc, "wd": wdsrc}[kind]
                s_, t_ = load_cb_raw(k, blk, src[e, :, :, cbg * 512:(cbg + 1) * 512],
                                     cb, buf, WSEMP[bi])

                def _(sy, s_=s_, t_=t_, kind=kind, e=e, cbg=cbg, cb=cb):
                    k.w(sy, "s", s_, t_)
                    k.i(CWSEM[bi], sy.dma_start(
                        out=dram["c" + kind][e][cbg], in_=cb[:, :]), 16)
                blk.sync(_)
                cache_done[tag] = True
                buf.mark(CWSEM[bi], k.cnt[CWSEM[bi]])
                return (s_, t_)
            ek = "s" if queue == "s" else "a"

            def _(eng, ek=ek):
                for cwn in CWSEM.values():
                    k.w(eng, ek, cwn)
                buf.wait_free(k, eng, ek)
                k.i(wsem, eng.dma_start(out=cb[:, :],
                                        in_=dram["c" + kind][e][cbg]), 16)
            (blk.sync if queue == "s" else blk.scalar)(_)
            return (wsem, k.cnt[wsem])

        for tt in range(4):
            trow = tt * 512
            tp_state = {}
            def _(sy, trow=trow):
                xacc_buf.wait_free(k, sy, "s")
                for tc in range(4):
                    row0 = trow + tc * 128
                    k.i("xL", sy.dma_start(
                        out=x_acc[:, tc * 2048:(tc + 1) * 2048],
                        in_=x_in[row0:row0 + 128, :]), 16)
            blk.sync(_)
            x_t = k.cnt["xL"]

            for tc in range(4):
                def _(v, x_t=x_t):
                    k.w(v, "v", "xL", x_t)
                    htm_buf.wait_free(k, v, "v")
                blk.vector(_)
                ln_t = emit_ln_tile(k, blk, x_acc[:, tc * 2048:(tc + 1) * 2048],
                                    h_tm[:, :], scr)

                def hw(dc, tc=tc):
                    return h_fm[:, dc * 512 + tc * 128:dc * 512 + (tc + 1) * 128]

                def _(t, ln_t=ln_t, pev=prev_expert_evac):
                    k.w(t, "t", "dv", ln_t)
                    k.w(t, "t", "dv", pev)
                blk.tensor(_)
                emit_transpose_tile(k, blk, h_tm, hw, tp_ps, tp_state, ident_bf)
                htm_buf.mark("pe", k.cnt["pe"])
            hfm_ready = k.cnt["dv"]

            inh_t = [0] * 4
            for tc in range(4):
                def _(t, tc=tc):
                    k.w(t, "t", "dv", hfm_ready)
                    k.w(t, "t", "cst", smalls_t)
                    if tc > 0:
                        k.w(t, "t", "ac", inh_t[tc - 1])
                    for dc in range(NDC):
                        ins = t.matmul(
                            pd0[:, 0:8],
                            h_fm[:, dc * 512 + tc * 128:dc * 512 + (tc + 1) * 128],
                            wr_bf[:, dc * 8:(dc + 1) * 8],
                            start=(dc == 0), stop=(dc == NDC - 1))
                    k.i("pe", ins)
                blk.tensor(_)
                lg_t = k.cnt["pe"]

                def _(sc, tc=tc, lg_t=lg_t):
                    k.w(sc, "a", "pe", lg_t)
                    k.w(sc, "a", "dv", tinv_t)
                    k.i("ac", sc.activation(rsc[:, tc * 8:(tc + 1) * 8],
                                            pd0[:, 0:8], AF.Sigmoid,
                                            scale=tinv[:, 0:1]))
                blk.scalar(_)
                inh_t[tc] = k.cnt["ac"]

            def _(v):
                k.w(v, "v", "ac", inh_t[3])
                for tc in range(4):
                    k.i("dv", v.tensor_scalar(rsc[:, 32 + tc * 8:32 + (tc + 1) * 8],
                                              rsc[:, tc * 8:(tc + 1) * 8],
                                              GOLD_C, None, ALU.subtract))
            blk.vector(_)

            def _(sc):
                k.w(sc, "a", "dv")
                for tc in range(4):
                    k.i("ac", sc.activation(rsc[:, 32 + tc * 8:32 + (tc + 1) * 8],
                                            rsc[:, 32 + tc * 8:32 + (tc + 1) * 8],
                                            AF.Abs))
                k.w(sc, "a", "ac")
                for tc in range(4):
                    k.i("ac", sc.activation(rsc[:, 64 + tc * 8:64 + (tc + 1) * 8],
                                            rsc[:, 32 + tc * 8:32 + (tc + 1) * 8],
                                            AF.Exp, scale=-10.0))
                for tc in range(4):
                    k.i("ac", sc.activation(rsc[:, 96 + tc * 8:96 + (tc + 1) * 8],
                                            rsc[:, 32 + tc * 8:32 + (tc + 1) * 8],
                                            AF.Exp, scale=-(1.0 / 0.3)))
            blk.scalar(_)
            acts_t = k.cnt["ac"]

            for tc in range(4):
                _emit_router_tc(k, blk, rsc, rs1, w_tm, tc, acts_t)
            wt_t = k.cnt["dv"]
            router_ac = k.cnt["ac"]

            for tc in range(4):
                def _(t, tc=tc, wt_t=wt_t):
                    k.w(t, "t", "dv", wt_t)
                    k.i("pe", t.transpose(pd1[0:8, 0:128],
                                          w_tm[:, tc * 8:(tc + 1) * 8],
                                          id32[:, :]))
                blk.tensor(_)
                t_t = k.cnt["pe"]

                def _(v, tc=tc, t_t=t_t):
                    k.w(v, "v", "pe", t_t)
                    k.i("dv", v.tensor_copy(w_fm[:, tc * 128:(tc + 1) * 128],
                                            pd1[0:8, 0:128]))
                blk.vector(_)

                def _(t, tgt=k.cnt["dv"]):
                    k.w(t, "t", "dv", tgt)
                blk.tensor(_)
            wfm_t = k.cnt["dv"]

            pend = {}
            gu_evac = {}
            pd_evac = {}
            gu_state = {}

            def preload_pair(e, cbg):
                p2 = cbg % 2
                sA = load_cb("wg", e, cbg, 2 * p2, "s")
                sB = load_cb("wu", e, cbg, 2 * p2 + 1, "a")
                pend[(e, "gu", cbg)] = (sA, sB, p2)

            def preload_wd(e, j):
                bi = j % 4
                sD = load_cb("wd", e, j, bi, "s" if j % 2 == 0 else "a")
                pend[(e, "d", j)] = (sD, bi)

            preload_pair(0, 0)
            for e in range(NE):
                is_lo = e < 2

                def _(t, e=e, pev=prev_expert_evac):
                    k.w(t, "t", "dv", wfm_t)
                    k.w(t, "t", "dv", pev)
                    k.i("pe", t.matmul(pw[:, :], sel_sb[:, e * 128:(e + 1) * 128],
                                       w_fm[:, :], start=True, stop=True))
                blk.tensor(_)
                pw_t = k.cnt["pe"]

                def _(v, pw_t=pw_t):
                    k.w(v, "v", "pe", pw_t)
                    k.i("dv", v.tensor_copy(pw_sb[:, :], pw[:, :]))
                blk.vector(_)
                pw_cp = k.cnt["dv"]

                ya_done = 0
                if is_lo:
                    for asb, ysb, pp in ((ga_sb, ya_g, pg0), (ua_sb, ya_u, pu0)):
                        def _(t, e=e, asb=asb, pp=pp, pev=prev_expert_evac):
                            k.w(t, "t", "cst", smalls_t)
                            k.w(t, "t", "dv", hfm_ready)
                            k.w(t, "t", "dv", pev)
                            prev0 = gu_evac.get(0)
                            if prev0:
                                k.w(t, "t", "dv", prev0)
                            k.w(t, "t", "ac", router_ac)
                            for dc in range(NDC):
                                ins = t.matmul(
                                    pp[0:64, :],
                                    asb[:, e * 1024 + dc * RLO:
                                        e * 1024 + (dc + 1) * RLO],
                                    h_fm[:, dc * 512:(dc + 1) * 512],
                                    start=(dc == 0), stop=(dc == NDC - 1))
                            k.i("pe", ins)
                        blk.tensor(_)
                        ya_t = k.cnt["pe"]

                        def _(v, ysb=ysb, pp=pp, ya_t=ya_t):
                            k.w(v, "v", "pe", ya_t)
                            k.i("dv", v.tensor_copy(ysb[:, :], pp[0:64, :]))
                        blk.vector(_)

                        def _(t, tgt=k.cnt["dv"]):
                            k.w(t, "t", "dv", tgt)
                        blk.tensor(_)
                    ya_done = k.cnt["dv"]

                for cbg in range(4):
                    sA, sB, p2 = pend.pop((e, "gu", cbg))
                    if cbg + 1 < 4:
                        preload_pair(e, cbg + 1)
                    else:
                        preload_wd(e, 0)
                    for fi in range(4):
                        fc = cbg * 4 + fi
                        _emit_gu_mm(k, blk, e, fc, fi, cbs[2 * p2],
                                    cbs[2 * p2 + 1], sA, sB, h_fm, hfm_ready,
                                    pg, pu, gb_sb, ub_sb, ya_g, ya_u, ya_done,
                                    c1, c2, is_lo, prev_expert_evac, router_ac,
                                    gu_evac, gu_state)
                    _emit_gu_tail(k, blk, cbg, c1, c2, pw_sb, mid, pw_cp,
                                  gu_state)
                    cbbuf[2 * p2].mark("pe", k.cnt["pe"])
                    cbbuf[2 * p2 + 1].mark("pe", k.cnt["pe"])
                mid_ready = k.cnt["dv"]

                za_done = 0
                if is_lo:
                    def _(t, e=e, mid_ready=mid_ready):
                        k.w(t, "t", "dv", mid_ready)
                        k.w(t, "t", "ac", router_ac)
                        pv_ = pd_evac.get(0)
                        if pv_:
                            k.w(t, "t", "dv", pv_)
                        for fc in range(NDC):
                            ins = t.matmul(
                                pd0[0:64, :],
                                da_sb[:, e * 1024 + fc * RLO:
                                      e * 1024 + (fc + 1) * RLO],
                                mid[:, fc * 512:(fc + 1) * 512],
                                start=(fc == 0), stop=(fc == NDC - 1))
                        k.i("pe", ins)
                    blk.tensor(_)
                    za_t = k.cnt["pe"]

                    def _(v, za_t=za_t):
                        k.w(v, "v", "pe", za_t)
                        k.i("dv", v.tensor_copy(za[:, :], pd0[0:64, :]))
                    blk.vector(_)

                    def _(t, tgt=k.cnt["dv"]):
                        k.w(t, "t", "dv", tgt)
                    blk.tensor(_)
                    za_done = k.cnt["dv"]

                for j in range(4):
                    sD, bi = pend.pop((e, "d", j))
                    if j + 1 < 4:
                        preload_wd(e, j + 1)
                    elif e + 1 < NE:
                        preload_pair(e + 1, 0)
                    for tc in range(4):
                        pdx = pd[(j * 4 + tc) % 2]

                        def _(t, e=e, j=j, tc=tc, pdx=pdx, sD=sD, bi=bi,
                              mid_ready=mid_ready, za_done=za_done):
                            k.w(t, "t", sD[0], sD[1])
                            k.w(t, "t", "dv", mid_ready)
                            k.w(t, "t", "ac", router_ac)
                            pv_ = pd_evac.get((j * 4 + tc) % 2)
                            if pv_:
                                k.w(t, "t", "dv", pv_)
                            if e < 2:
                                k.w(t, "t", "dv", za_done)
                            for fc in range(NDC):
                                last = (fc == NDC - 1) and (e >= 2)
                                ins = t.matmul(
                                    pdx[:, :],
                                    mid[:, fc * 512 + tc * 128:
                                        fc * 512 + (tc + 1) * 128],
                                    cbs[bi][:, fc * 512:(fc + 1) * 512],
                                    start=(fc == 0), stop=last)
                            if e < 2:
                                ins = t.matmul(
                                    pdx[:, :], za[:, tc * 128:(tc + 1) * 128],
                                    db_sb[:, e * 2048 + j * 512:
                                          e * 2048 + (j + 1) * 512],
                                    start=False, stop=True)
                            k.i("pe", ins)
                        blk.tensor(_)
                        dm_t = k.cnt["pe"]

                        def _(v, j=j, tc=tc, pdx=pdx, dm_t=dm_t):
                            k.w(v, "v", "pe", dm_t)
                            dst = x_acc[:, tc * 2048 + j * 512:
                                        tc * 2048 + (j + 1) * 512]
                            k.i("dv", v.tensor_tensor(dst, dst, pdx[:, :],
                                                      ALU.add))
                        blk.vector(_)
                        pd_evac[(j * 4 + tc) % 2] = k.cnt["dv"]
                    cbbuf[bi].mark("pe", k.cnt["pe"])
                prev_expert_evac = k.cnt["dv"]

            acc_t = k.cnt["dv"]

            def _(sy, trow=trow, acc_t=acc_t):
                k.w(sy, "s", "dv", acc_t)
                for tc in range(4):
                    k.i("stX", sy.dma_start(
                        out=x_out[trow + tc * 128:trow + (tc + 1) * 128, :],
                        in_=x_acc[:, tc * 2048:(tc + 1) * 2048]), 16)
            blk.sync(_)
            xacc_buf.mark("stX", k.cnt["stX"])
        drain_dmas(k, blk)


def _emit_router_tc(k, blk, rsc, rs1, w_tm, tc, acts_t):
    inh = rsc[:, tc * 8:(tc + 1) * 8]
    dist = rsc[:, 32 + tc * 8:32 + (tc + 1) * 8]
    wexp = rsc[:, 64 + tc * 8:64 + (tc + 1) * 8]
    fbt = rsc[:, 96 + tc * 8:96 + (tc + 1) * 8]

    def _(v):
        k.w(v, "v", "ac", acts_t)
        k.w(v, "v", "dv")
        k.i("dv", v.tensor_scalar(rs1[:, 0:8], inh, GOLD_L, None, ALU.is_ge))
        k.w(v, "v", "dv")
        k.i("dv", v.tensor_scalar(dist, inh, GOLD_U, None, ALU.is_le))
        k.w(v, "v", "dv")
        k.i("dv", v.tensor_tensor(rs1[:, 0:8], rs1[:, 0:8], dist, ALU.mult))
        k.w(v, "v", "dv")
        k.i("dv", v.tensor_tensor(wexp, wexp, rs1[:, 0:8], ALU.mult))
        k.w(v, "v", "dv")
        k.i("dv", v.tensor_reduce(rs1[:, 0:1], wexp, AX.X, ALU.add))
        k.w(v, "v", "dv")
        k.i("dv", v.tensor_scalar(rs1[:, 1:2], rs1[:, 0:1], 1e-8, None,
                                  ALU.is_lt))
        k.i("dv", v.tensor_reduce(rs1[:, 2:3], fbt, AX.X, ALU.max))
        k.w(v, "v", "dv")
        k.i("dv", v.tensor_scalar(dist, fbt, rs1[:, 2:3], None, ALU.is_equal))
        k.w(v, "v", "dv")
        k.i("dv", v.scalar_tensor_tensor(inh, dist, NEG_BIG, fbt, ALU.mult,
                                         ALU.add))
        k.w(v, "v", "dv")
        k.i("dv", v.tensor_reduce(rs1[:, 3:4], inh, AX.X, ALU.max))
        k.w(v, "v", "dv")
        k.i("dv", v.tensor_scalar(inh, fbt, rs1[:, 3:4], None, ALU.is_equal))
        k.w(v, "v", "dv")
        k.i("dv", v.tensor_tensor(dist, dist, inh, ALU.add))
        k.w(v, "v", "dv")
        k.i("dv", v.tensor_tensor(fbt, fbt, dist, ALU.mult))
        k.w(v, "v", "dv")
        k.i("dv", v.tensor_reduce(rs1[:, 4:5], fbt, AX.X, ALU.add))
        k.w(v, "v", "dv")
        k.i("dv", v.tensor_scalar(rs1[:, 4:5], rs1[:, 4:5], 1e-8, None, ALU.max))
        k.w(v, "v", "dv")
        k.i("dv", v.reciprocal(rs1[:, 4:5], rs1[:, 4:5]))
        k.w(v, "v", "dv")
        k.i("dv", v.tensor_scalar(fbt, fbt, rs1[:, 4:5], None, ALU.mult))
        k.w(v, "v", "dv")
        k.i("dv", v.tensor_tensor(dist, fbt, wexp, ALU.subtract))
        k.w(v, "v", "dv")
        k.i("dv", v.scalar_tensor_tensor(wexp, dist, rs1[:, 1:2], wexp,
                                         ALU.mult, ALU.add))
        k.w(v, "v", "dv")
        k.i("dv", v.tensor_reduce(rs1[:, 5:6], wexp, AX.X, ALU.add))
        k.w(v, "v", "dv")
        k.i("dv", v.tensor_scalar(rs1[:, 5:6], rs1[:, 5:6], 1e-8, None, ALU.max))
        k.w(v, "v", "dv")
        k.i("dv", v.reciprocal(rs1[:, 5:6], rs1[:, 5:6]))
        k.w(v, "v", "dv")
        k.i("dv", v.tensor_scalar(w_tm[:, tc * 8:(tc + 1) * 8], wexp,
                                  rs1[:, 5:6], None, ALU.mult))
    blk.vector(_)


def _emit_gu_mm(k, blk, e, fc, fi, cbG, cbU, sA, sB, h_fm, hfm_ready, pg, pu,
                gb_sb, ub_sb, ya_g, ya_u, ya_done, c1, c2, is_lo,
                prev_expert_evac, router_ac, gu_evac, gu_state):
    par = fc % 2

    def _(t):
        k.w(t, "t", sA[0], sA[1])
        k.w(t, "t", sB[0], sB[1])
        k.w(t, "t", "dv", hfm_ready)
        k.w(t, "t", "dv", prev_expert_evac)
        k.w(t, "t", "ac", router_ac)
        prev = gu_evac.get(par)
        if prev:
            k.w(t, "t", "dv", prev)
        if is_lo:
            k.w(t, "t", "dv", ya_done)
        for dc in range(NDC):
            ins = t.matmul(pg[par][:, :],
                           cbG[:, dc * 512 + fi * 128:dc * 512 + (fi + 1) * 128],
                           h_fm[:, dc * 512:(dc + 1) * 512],
                           start=(dc == 0),
                           stop=(not is_lo and dc == NDC - 1))
        if is_lo:
            ins = t.matmul(pg[par][:, :],
                           gb_sb[:, e * 2048 + fc * 128:e * 2048 + (fc + 1) * 128],
                           ya_g[:, :], start=False, stop=True)
        for dc in range(NDC):
            ins = t.matmul(pu[par][:, :],
                           cbU[:, dc * 512 + fi * 128:dc * 512 + (fi + 1) * 128],
                           h_fm[:, dc * 512:(dc + 1) * 512],
                           start=(dc == 0),
                           stop=(not is_lo and dc == NDC - 1))
        if is_lo:
            ins = t.matmul(pu[par][:, :],
                           ub_sb[:, e * 2048 + fc * 128:e * 2048 + (fc + 1) * 128],
                           ya_u[:, :], start=False, stop=True)
        k.i("pe", ins)
    blk.tensor(_)
    mm_t = k.cnt["pe"]

    def _(v):
        k.w(v, "v", "pe", mm_t)
        rel = gu_state.get(("crel", fi))
        if rel:
            k.w(v, "v", "dv", rel)
        k.i("dv", v.tensor_copy(c1[:, fi * 512:(fi + 1) * 512], pg[par][:, :]))
        k.w(v, "v", "dv")
        k.i("dv", v.tensor_copy(c2[:, fi * 512:(fi + 1) * 512], pu[par][:, :]))
    blk.vector(_)
    gu_evac[par] = k.cnt["dv"]
    gu_state[("cp", fc)] = k.cnt["dv"]


def _emit_gu_tail(k, blk, cbg, c1, c2, pw_sb, mid, pw_cp, gu_state):
    fis = list(range(4))

    def _(sc):
        for fi in fis:
            fc = cbg * 4 + fi
            k.w(sc, "a", "dv", gu_state[("cp", fc)])
            k.i("ac", sc.activation(c1[:, fi * 512:(fi + 1) * 512],
                                    c1[:, fi * 512:(fi + 1) * 512], AF.Silu))
            gu_state[("silu", fc)] = k.cnt["ac"]
    blk.scalar(_)

    def _(v):
        k.w(v, "v", "dv", pw_cp)
        for fi in fis:
            fc = cbg * 4 + fi
            k.w(v, "v", "ac", gu_state[("silu", fc)])
            k.i("dv", v.tensor_tensor(c2[:, fi * 512:(fi + 1) * 512],
                                      c1[:, fi * 512:(fi + 1) * 512],
                                      c2[:, fi * 512:(fi + 1) * 512], ALU.mult))
            k.w(v, "v", "dv")
            k.i("dv", v.tensor_tensor(mid[:, fc * 512:(fc + 1) * 512],
                                      c2[:, fi * 512:(fi + 1) * 512],
                                      pw_sb[:, :], ALU.mult))
            gu_state[("crel", fi)] = k.cnt["dv"]
    blk.vector(_)



def phase_B1(nc, k, g, dram):
    with ExitStack() as ctx:
        E = ctx.enter_context
        xb = E(nc.sbuf_tensor("b1_xb", [128, 2048], F32))
        hf32 = E(nc.sbuf_tensor("b1_h", [128, 2048], F32))
        h2 = E(nc.sbuf_tensor("b1_h2", [128, 2048], BF16))
        xn_fm = E(nc.sbuf_tensor("b1_xn", [128, 8192], BF16))
        cb0 = E(nc.sbuf_tensor("b1_cb0", [128, 8192], BF16))
        cb1 = E(nc.sbuf_tensor("b1_cb1", [128, 8192], BF16))
        qo = E(nc.sbuf_tensor("b1_qo", [128, 512], BF16))
        stats = E(nc.sbuf_tensor("b1_st", [128, 24], F32))
        mv = E(nc.sbuf_tensor("b1_mv", [128, 2], F32))
        vv = E(nc.sbuf_tensor("b1_vv", [128, 1], F32))
        rstd = E(nc.sbuf_tensor("b1_rs", [128, 1], F32))
        tp0 = E(nc.psum_tensor("b1_tp0", [128, 1024], BF16))
        tp1 = E(nc.psum_tensor("b1_tp1", [128, 1024], BF16))
        pqa = E(nc.psum_tensor("b1_pqa", [128, 512], F32))
        pqb = E(nc.psum_tensor("b1_pqb", [128, 512], F32))
        pv = E(nc.psum_tensor("b1_pv", [128, 512], F32))
        blk = E(nc.Block("B1"))

        scr = {"stats": stats, "mv": mv, "vv": vv, "rstd": rstd}
        K_fm, V_tm = g["K_fm"], g["V_tm"]
        ident_bf = g["ident_bf"]
        gbg, gbb = g["gbg"], g["gbb"]
        qsrc = dram["qkv"].rearrange("(dc p) f -> p dc f", p=128)
        pqs = (pqa, pqb)
        pq_evac = {}
        pv_evac = [0]
        cbbuf = [Buf(), Buf()]
        xb_buf = Buf()
        h_buf = Buf()
        h2_buf = Buf()
        qo_buf = Buf()
        tp_state = {}

        for s in range(4):
            scol = s * 512
            for ti in range(4):
                row0 = scol + ti * 128

                def _(sy, row0=row0):
                    xb_buf.wait_free(k, sy, "s")
                    k.i("xL", sy.dma_start(out=xb[:, :],
                                           in_=dram["XB"][row0:row0 + 128, :]), 16)
                blk.sync(_)
                xld = k.cnt["xL"]

                def _(v, xld=xld):
                    k.w(v, "v", "xL", xld)
                    h_buf.wait_free(k, v, "v")
                blk.vector(_)
                emit_ln_tile(k, blk, xb[:, :], hf32[:, :], scr)
                xb_buf.mark("dv", k.cnt["dv"])
                h_t = k.cnt["dv"]

                def _(sy, row0=row0, h_t=h_t):
                    k.w(sy, "s", "dv", h_t)
                    k.i("hS", sy.dma_start(out=dram["Hb"][row0:row0 + 128, :],
                                           in_=hf32[:, :]), 16)
                blk.sync(_)
                h_buf.mark("hS", k.cnt["hS"])

                def _(v):
                    h2_buf.wait_free(k, v, "v")
                blk.vector(_)
                ln2_t = emit_ln_tile(k, blk, hf32[:, :], h2[:, :], scr)
                h_buf.mark("dv", ln2_t)

                def hw(dc, ti=ti):
                    return xn_fm[:, dc * 512 + ti * 128:dc * 512 + (ti + 1) * 128]

                def _(t, ln2_t=ln2_t):
                    k.w(t, "t", "dv", ln2_t)
                blk.tensor(_)
                emit_transpose_tile(k, blk, h2, hw, [tp0, tp1], tp_state,
                                    ident_bf,
                                    affine=(lambda dc: gbg[:, dc:dc + 1],
                                            lambda dc: gbb[:, dc:dc + 1]))
                h2_buf.mark("pe", k.cnt["pe"])
            xn_ready = k.cnt["dv"]

            for r, rname in ((1, "k"), (2, "v"), (0, "q")):
                for cbg in range(4):
                    cbp = (r * 4 + cbg) % 2
                    cb = (cb0, cb1)[cbp]
                    sem, tgt = load_cb_raw(
                        k, blk,
                        qsrc[:, :, r * 2048 + cbg * 512:r * 2048 + (cbg + 1) * 512],
                        cb, cbbuf[cbp], WSEMP[cbp])
                    if rname in ("k", "q"):
                        for fi in range(4):
                            fc = cbg * 4 + fi
                            par = fc % 2
                            pq = pqs[par]

                            def _(t, fc=fc, fi=fi, cb=cb, sem=sem, tgt=tgt,
                                  xn_ready=xn_ready, par=par, pq=pq):
                                k.w(t, "t", sem, tgt)
                                k.w(t, "t", "dv", xn_ready)
                                prev = pq_evac.get(par)
                                if prev:
                                    k.w(t, "t", "dv", prev)
                                for dc in range(NDC):
                                    ins = t.matmul(
                                        pq[:, :],
                                        cb[:, dc * 512 + fi * 128:
                                           dc * 512 + (fi + 1) * 128],
                                        xn_fm[:, dc * 512:(dc + 1) * 512],
                                        start=(dc == 0), stop=(dc == NDC - 1))
                                k.i("pe", ins)
                            blk.tensor(_)
                            mm_t = k.cnt["pe"]
                            if rname == "k":
                                def _(v, fc=fc, scol=scol, mm_t=mm_t, pq=pq):
                                    k.w(v, "v", "pe", mm_t)
                                    k.i("dv", v.tensor_copy(
                                        K_fm[:, fc * 2048 + scol:
                                             fc * 2048 + scol + 512], pq[:, :]))
                                blk.vector(_)
                                pq_evac[par] = k.cnt["dv"]
                            else:
                                def _(v, mm_t=mm_t, pq=pq):
                                    k.w(v, "v", "pe", mm_t)
                                    qo_buf.wait_free(k, v, "v")
                                    k.i("dv", v.tensor_copy(qo[:, :], pq[:, :]))
                                blk.vector(_)
                                q_t = k.cnt["dv"]
                                pq_evac[par] = q_t

                                def _(sy, fc=fc, scol=scol, q_t=q_t):
                                    k.w(sy, "s", "dv", q_t)
                                    k.i("qS", sy.dma_start(
                                        out=dram["Qb"][fc * 128:(fc + 1) * 128,
                                                       scol:scol + 512],
                                        in_=qo[:, :]), 16)
                                blk.sync(_)
                                qo_buf.mark("qS", k.cnt["qS"])
                    else:
                        for tc in range(4):
                            tcg = s * 4 + tc

                            def _(t, tc=tc, cb=cb, sem=sem, tgt=tgt,
                                  xn_ready=xn_ready):
                                k.w(t, "t", sem, tgt)
                                k.w(t, "t", "dv", xn_ready)
                                if pv_evac[0]:
                                    k.w(t, "t", "dv", pv_evac[0])
                                for dc in range(NDC):
                                    ins = t.matmul(
                                        pv[:, :],
                                        xn_fm[:, dc * 512 + tc * 128:
                                              dc * 512 + (tc + 1) * 128],
                                        cb[:, dc * 512:(dc + 1) * 512],
                                        start=(dc == 0), stop=(dc == NDC - 1))
                                k.i("pe", ins)
                            blk.tensor(_)
                            mm_t = k.cnt["pe"]

                            def _(v, tcg=tcg, cbg=cbg, mm_t=mm_t):
                                k.w(v, "v", "pe", mm_t)
                                k.i("dv", v.tensor_copy(
                                    V_tm[:, tcg * 2048 + cbg * 512:
                                         tcg * 2048 + (cbg + 1) * 512],
                                    pv[:, :]))
                            blk.vector(_)
                            pv_evac[0] = k.cnt["dv"]
                    cbbuf[cbp].mark("pe", k.cnt["pe"])
        drain_dmas(k, blk)



def phase_B2(nc, k, g, dram):
    with ExitStack() as ctx:
        E = ctx.enter_context
        qb = E(nc.sbuf_tensor("b2_qb", [128, 8192], BF16))
        p_sb = E(nc.sbuf_tensor("b2_p", [128, 2048], BF16))
        pT = E(nc.sbuf_tensor("b2_pt", [128, 2048], BF16))
        osb = E(nc.sbuf_tensor("b2_o", [128, 2048], BF16))
        bm = E(nc.sbuf_tensor("b2_bm", [128, 2048], F32))
        mx = E(nc.sbuf_tensor("b2_mx", [128, 1], F32))
        nb2 = E(nc.sbuf_tensor("b2_nb", [128, 1], F32))
        rs = E(nc.sbuf_tensor("b2_rs", [128, 1], F32))
        psS = E(nc.psum_tensor("b2_ps", [128, 2048], F32))
        pO0 = E(nc.psum_tensor("b2_po0", [128, 512], F32))
        pO1 = E(nc.psum_tensor("b2_po1", [128, 512], F32))
        ptp0 = E(nc.psum_tensor("b2_tp0", [128, 1024], BF16))
        ptp1 = E(nc.psum_tensor("b2_tp1", [128, 1024], BF16))
        blk = E(nc.Block("B2"))

        K_fm, V_tm = g["K_fm"], g["V_tm"]
        ident_bf = g["ident_bf"]
        qb_buf = Buf()
        o_buf = Buf()
        pO = (pO0, pO1)
        ptp = (ptp0, ptp1)

        def _(gp):
            for m in range(4):
                k.i("cst", gp.dma_start(out=bm[:, m * 512:(m + 1) * 512],
                                        in_=dram["bmc"][m]), 16)
        blk.gpsimd(_)
        bm_t = k.cnt["cst"]
        prev_exp = 0
        prev_tp_pe = 0

        for b in range(4):
            def _(sa, b=b):
                qb_buf.wait_free(k, sa, "a")
                for dc in range(NDC):
                    k.i("qL0", sa.dma_start(
                        out=qb[:, dc * 512:(dc + 1) * 512],
                        in_=dram["Qb"][dc * 128:(dc + 1) * 128,
                                       b * 512:(b + 1) * 512]), 16)
            blk.scalar(_)
            q_t = k.cnt["qL0"]

            for si in range(4):
                NB = b + 1
                Tc = NB * 512

                def _(t, b=b, si=si, q_t=q_t, prev_exp=prev_exp):
                    k.w(t, "t", "qL0", q_t)
                    k.w(t, "t", "cst", bm_t)
                    k.w(t, "t", "ac", prev_exp)
                    for tb in range(NB):
                        for dc in range(NDC):
                            ins = t.matmul(
                                psS[:, tb * 512:(tb + 1) * 512],
                                qb[:, dc * 512 + si * 128:dc * 512 + (si + 1) * 128],
                                K_fm[:, dc * 2048 + tb * 512:
                                     dc * 2048 + (tb + 1) * 512],
                                start=(dc == 0), stop=(dc == NDC - 1))
                    k.i("pe", ins)
                blk.tensor(_)
                scm_t = k.cnt["pe"]

                def _(v, b=b, si=si, scm_t=scm_t, Tc=Tc):
                    k.w(v, "v", "pe", scm_t)
                    k.w(v, "v", "ac")
                    k.i("dv", v.tensor_tensor(psS[:, b * 512:(b + 1) * 512],
                                              psS[:, b * 512:(b + 1) * 512],
                                              bm[:, si * 512:(si + 1) * 512],
                                              ALU.add))
                    k.w(v, "v", "dv")
                    k.i("dv", v.tensor_reduce(mx[:, :], psS[:, 0:Tc], AX.X,
                                              ALU.max))
                    k.w(v, "v", "dv")
                    k.i("dv", v.tensor_scalar(nb2[:, :], mx[:, :], -CSCALE_B,
                                              None, ALU.mult))
                blk.vector(_)
                mx_t = k.cnt["dv"]

                def _(sc, Tc=Tc, mx_t=mx_t, prev_tp_pe=prev_tp_pe):
                    k.w(sc, "a", "dv", mx_t)
                    k.w(sc, "a", "pe", prev_tp_pe)
                    k.i("ac", sc.activation(p_sb[:, 0:Tc], psS[:, 0:Tc], AF.Exp,
                                            scale=CSCALE_B, bias=nb2[:, 0:1],
                                            accum_out=rs[:, 0:1]))
                blk.scalar(_)
                exp_t = k.cnt["ac"]
                prev_exp = exp_t

                def _(v, Tc=Tc, exp_t=exp_t):
                    k.w(v, "v", "ac", exp_t)
                    k.i("dv", v.reciprocal(rs[:, :], rs[:, :]))
                    k.w(v, "v", "dv")
                    k.i("dv", v.tensor_scalar(p_sb[:, 0:Tc], p_sb[:, 0:Tc],
                                              rs[:, 0:1], None, ALU.mult))
                blk.vector(_)
                pn_t = k.cnt["dv"]

                ntiles = NB * 4
                tp_state = {}
                for j in range(ntiles):
                    bank = j % 2

                    def _(t, j=j, bank=bank, pn_t=pn_t):
                        k.w(t, "t", "dv", pn_t)
                        if bank in tp_state:
                            nm, v_ = tp_state[bank]
                            k.w(t, "t", nm, v_)
                        k.i("pe", t.transpose(
                            ptp[bank][:, 0:128],
                            p_sb[:, j * 128:(j + 1) * 128], ident_bf[:, :]))
                    blk.tensor(_)
                    t_t = k.cnt["pe"]

                    def _(v, j=j, bank=bank, t_t=t_t):
                        k.w(v, "v", "pe", t_t)
                        tgt = k.i("dv", v.tensor_copy(
                            pT[:, j * 128:(j + 1) * 128],
                            ptp[bank][:, 0:128]))
                        tp_state[bank] = ("dv", tgt)
                    blk.vector(_)
                pT_t = k.cnt["dv"]
                prev_tp_pe = k.cnt["pe"]

                for dc in range(NDC):
                    pp = pO[dc % 2]

                    def _(t, dc=dc, pp=pp, ntiles=ntiles, pT_t=pT_t):
                        k.w(t, "t", "dv", pT_t)
                        for j in range(ntiles):
                            ins = t.matmul(
                                pp[:, 0:128],
                                V_tm[:, j * 2048 + dc * 128:
                                     j * 2048 + (dc + 1) * 128],
                                pT[:, j * 128:(j + 1) * 128],
                                start=(j == 0), stop=(j == ntiles - 1))
                        k.i("pe", ins)
                    blk.tensor(_)
                    av_t = k.cnt["pe"]

                    def _(v, dc=dc, pp=pp, av_t=av_t):
                        k.w(v, "v", "pe", av_t)
                        if dc == 0:
                            o_buf.wait_free(k, v, "v")
                        k.i("dv", v.tensor_copy(osb[:, dc * 128:(dc + 1) * 128],
                                                pp[:, 0:128]))
                    blk.vector(_)

                    def _(t, tgt=k.cnt["dv"]):
                        k.w(t, "t", "dv", tgt)
                    blk.tensor(_)
                o_t = k.cnt["dv"]

                def _(sy, b=b, si=si, o_t=o_t):
                    k.w(sy, "s", "dv", o_t)
                    k.i("oS", sy.dma_start(
                        out=dram["Ob"][b].rearrange(
                            "p (dc f) -> p dc f", f=512)[:, :, si * 128:(si + 1) * 128],
                        in_=osb[:, :].rearrange("p (dc f) -> p dc f", f=128)), 16)
                blk.sync(_)
                o_buf.mark("oS", k.cnt["oS"])
            qb_buf.mark("pe", k.cnt["pe"])
        drain_dmas(k, blk)



def build_program(n_sub=4, dbg=()):
    nc = bass.Bass("TRN2", target_bir_lowering=False, debug=False)
    dram = {}

    def dt(name, shape, dtype, kind):
        return nc.dram_tensor(name, shape, dtype, kind=kind).ap()

    dram["hs"] = dt("hs", [T, D], F32, "ExternalInput")
    for nm in ("wqt", "wkt", "wvt", "wot", "opj"):
        dram[nm] = dt(nm, [D, D], F32, "ExternalInput")
    dram["qkv"] = dt("qkv", [D, 3 * D], F32, "ExternalInput")
    for nm in ("wg", "wu", "wd"):
        dram[nm] = dt(nm, [NE, D, D], F32, "ExternalInput")
    for nm in ("ga", "ua", "da"):
        dram[nm] = dt(nm, [2, D, RLO], F32, "ExternalInput")
    for nm in ("gb", "ub", "db"):
        dram[nm] = dt(nm, [2, RLO, D], F32, "ExternalInput")
    dram["wrp"] = dt("wrp", [128, 128], F32, "ExternalInput")
    dram["temp"] = dt("temp", [1, 1], F32, "ExternalInput")
    dram["gam_a_in"] = dt("gam_a_in", [128, 1], F32, "ExternalInput")
    dram["bet_a_in"] = dt("bet_a_in", [128, 1], F32, "ExternalInput")
    dram["gbg_in"] = dt("gbg_in", [128, 16], F32, "ExternalInput")
    dram["gbb_in"] = dt("gbb_in", [128, 16], F32, "ExternalInput")
    dram["identc"] = dt("identc", [128, 128], F32, "ExternalInput")
    dram["onesc"] = dt("onesc", [128, 128], F32, "ExternalInput")
    dram["epsc"] = dt("epsc", [1, 512], F32, "ExternalInput")
    amc = dt("amc", [4, 128, 512], F32, "ExternalInput")
    bmc = dt("bmc", [4, 128, 512], F32, "ExternalInput")
    dram["selc"] = dt("selc", [8, 1024], F32, "ExternalInput")
    dram["amc"] = [amc[m] for m in range(4)]
    dram["bmc"] = [bmc[m] for m in range(4)]
    dram["out"] = dt("out", [T, D], F32, "ExternalOutput")

    def internal(name, shape, dtype):
        kind = "ExternalOutput" if name in dbg else "Internal"
        return dt(name, shape, dtype, kind)

    dram["XA"] = internal("XA", [T, D], F32)
    dram["XB"] = internal("XB", [T, D], F32)
    dram["Qa"] = internal("Qa", [D, T], BF16)
    dram["Qb"] = internal("Qb", [D, T], BF16)
    dram["Hb"] = internal("Hb", [T, D], F32)
    oa = internal("Oa", [4, 128, 8192], BF16)
    ob = internal("Ob", [4, 128, 8192], BF16)
    dram["Oa"] = [oa[b] for b in range(4)]
    dram["Ob"] = [ob[b] for b in range(4)]
    for nm in ("cwg", "cwu", "cwd"):
        t_ = internal(nm, [NE, 4, 128, 8192], BF16)
        dram[nm] = [[t_[e][i] for i in range(4)] for e in range(NE)]

    cache_done = {}

    def tgt_name(i):
        if i == n_sub:
            return "out"
        return {1: "XA", 2: "XB", 3: "XA"}[i]

    with ExitStack() as ctx:
        E = ctx.enter_context
        k = KCtx(nc, ctx, SEMS)
        g = {}
        g["ident_bf"] = E(nc.sbuf_tensor("g_id", [128, 128], BF16))
        g["ones_bf"] = E(nc.sbuf_tensor("g_on", [128, 128], BF16))
        g["eps_bf"] = E(nc.sbuf_tensor("g_eps", [1, 512], BF16))
        g["gam_a"] = E(nc.sbuf_tensor("g_ga", [128, 1], F32))
        g["bet_a"] = E(nc.sbuf_tensor("g_ba", [128, 1], F32))
        g["gbg"] = E(nc.sbuf_tensor("g_gbg", [128, 16], F32))
        g["gbb"] = E(nc.sbuf_tensor("g_gbb", [128, 16], F32))

        with nc.Block("setup") as blk:
            def _(gp):
                k.i("cst", gp.dma_start(out=g["ident_bf"][:, :], in_=dram["identc"]), 16)
                k.i("cst", gp.dma_start(out=g["ones_bf"][:, :], in_=dram["onesc"]), 16)
                k.i("cst", gp.dma_start(out=g["eps_bf"][:, :], in_=dram["epsc"]), 16)
                k.i("cst", gp.dma_start(out=g["gam_a"][:, :], in_=dram["gam_a_in"]), 16)
                k.i("cst", gp.dma_start(out=g["bet_a"][:, :], in_=dram["bet_a_in"]), 16)
                k.i("cst", gp.dma_start(out=g["gbg"][:, :], in_=dram["gbg_in"]), 16)
                k.i("cst", gp.dma_start(out=g["gbb"][:, :], in_=dram["gbb_in"]), 16)
            blk.gpsimd(_)

            def _(sy):
                k.w(sy, "s", "cst")
            blk.sync(_)
        k.barrier()

        with ExitStack() as sA:
            g["K_fm"] = sA.enter_context(nc.sbuf_tensor("a_K", [128, 32768], BF16))
            g["V_tm"] = sA.enter_context(nc.sbuf_tensor("a_V", [128, 32768], BF16))
            phase_A1(nc, k, g, dram)
            k.barrier()
            phase_A2(nc, k, g, dram)
            k.barrier()
        phase_outproj(nc, k, g, dram, "A3", "Oa", "wot", "hs",
                      dram[tgt_name(1)])
        k.barrier()
        if n_sub >= 2:
            phase_FFN(nc, k, g, dram, cache_done, "F1", tgt_name(1),
                      dram[tgt_name(2)], 0)
            k.barrier()
        if n_sub >= 3:
            with ExitStack() as sB:
                g["K_fm"] = sB.enter_context(nc.sbuf_tensor("b_K", [128, 32768], BF16))
                g["V_tm"] = sB.enter_context(nc.sbuf_tensor("b_V", [128, 32768], BF16))
                phase_B1(nc, k, g, dram)
                k.barrier()
                phase_B2(nc, k, g, dram)
                k.barrier()
            phase_outproj(nc, k, g, dram, "B3", "Ob", "opj", "XB",
                          dram[tgt_name(3)], with_h="Hb")
            k.barrier()
        if n_sub >= 4:
            phase_FFN(nc, k, g, dram, cache_done, "F2", tgt_name(3),
                      dram[tgt_name(4)], 1)
            k.barrier()

        with nc.Block("fin") as blk:
            def _(sy):
                for nm in DMA_SEMS:
                    k.w(sy, "s", nm)
            blk.sync(_)

    return nc



def _host_consts():
    ident = np.eye(128, dtype=np.float32)
    ones = np.ones((128, 128), dtype=np.float32)
    epsr = np.full((1, 512), EPS, dtype=np.float32)
    r = np.arange(128)[:, None]
    c = np.arange(512)[None, :]
    am = np.stack([(r <= c - 128 * m).astype(np.float32) for m in range(4)])
    bm = np.stack([np.where(c <= 128 * m + r, 0.0, NEG_BIG).astype(np.float32)
                   for m in range(4)])
    sel = np.zeros((8, 1024), dtype=np.float32)
    for e in range(8):
        sel[e, e * 128:(e + 1) * 128] = 1.0
    return ident, ones, epsr, am, bm, sel


def make_common(inputs):
    ident, ones, epsr, am, bm, sel = _host_consts()
    wr = np.asarray(inputs["L0_ffn_Wr"], dtype=np.float32)
    wrp = np.ascontiguousarray(
        wr.reshape(16, 128, 8).transpose(1, 0, 2).reshape(128, 128))
    return {
        "wqt": np.ascontiguousarray(np.asarray(inputs["L0_attn_a_Wq"], dtype=np.float32).T),
        "wkt": np.ascontiguousarray(np.asarray(inputs["L0_attn_a_Wk"], dtype=np.float32).T),
        "wvt": np.ascontiguousarray(np.asarray(inputs["L0_attn_a_Wv"], dtype=np.float32).T),
        "wot": np.ascontiguousarray(np.asarray(inputs["L0_attn_a_Wo"], dtype=np.float32).T),
        "qkv": np.ascontiguousarray(np.asarray(inputs["L0_attn_b_qkv"], dtype=np.float32)),
        "opj": np.ascontiguousarray(np.asarray(inputs["L0_attn_b_o_proj"], dtype=np.float32)),
        "wg": np.ascontiguousarray(np.asarray(inputs["L0_ffn_Wg"], dtype=np.float32)),
        "wu": np.ascontiguousarray(np.asarray(inputs["L0_ffn_Wu"], dtype=np.float32)),
        "wd": np.ascontiguousarray(np.asarray(inputs["L0_ffn_Wd"], dtype=np.float32)),
        "ga": np.ascontiguousarray(np.asarray(inputs["L0_ffn_Ga"], dtype=np.float32)),
        "ua": np.ascontiguousarray(np.asarray(inputs["L0_ffn_Ua"], dtype=np.float32)),
        "da": np.ascontiguousarray(np.asarray(inputs["L0_ffn_Da"], dtype=np.float32)),
        "gb": np.ascontiguousarray(np.asarray(inputs["L0_ffn_Gb"], dtype=np.float32)),
        "ub": np.ascontiguousarray(np.asarray(inputs["L0_ffn_Ub"], dtype=np.float32)),
        "db": np.ascontiguousarray(np.asarray(inputs["L0_ffn_Db"], dtype=np.float32)),
        "wrp": wrp,
        "temp": np.asarray(inputs["L0_ffn_temperature"], dtype=np.float32).reshape(1, 1),
        "gam_a_in": np.asarray(inputs["L0_attn_a_gamma"], dtype=np.float32).reshape(128, 1).copy(),
        "bet_a_in": np.asarray(inputs["L0_attn_a_beta"], dtype=np.float32).reshape(128, 1).copy(),
        "gbg_in": np.ascontiguousarray(
            np.asarray(inputs["L0_attn_b_gamma"], dtype=np.float32).reshape(16, 128).T),
        "gbb_in": np.ascontiguousarray(
            np.asarray(inputs["L0_attn_b_beta"], dtype=np.float32).reshape(16, 128).T),
        "identc": ident, "onesc": ones, "epsc": epsr,
        "amc": am, "bmc": bm, "selc": sel,
    }


def kernel(**inputs):
    common = make_common(inputs)
    hs = np.asarray(inputs["L0_attn_a_hidden_states"], dtype=np.float32)
    nc = build_program(n_sub=4)
    in_maps = []
    for c in range(8):
        m = dict(common)
        m["hs"] = np.ascontiguousarray(hs[c])
        in_maps.append(m)
    res = run_bass_kernel_spmd(nc, in_maps, core_ids=list(range(8)))
    return np.stack([np.asarray(r["out"]) for r in res.results], axis=0)
